# version 03
# baseline (speedup 1.0000x reference)
import numpy as np
from contextlib import ExitStack

import concourse.bass as bass
import concourse.mybir as mybir
from concourse.bass_utils import run_bass_kernel_spmd

F32 = mybir.dt.float32
BF16 = mybir.dt.bfloat16
AX = mybir.AxisListType.X
ALU = mybir.AluOpType
ACTF = mybir.ActivationFunctionType

NCORE = 8
TC, NT, TS = 2048, 16, 128
NGR, GS = 4, 512
D, DC = 1024, 8
HF, NJ = 1408, 11
NSLOT = 10
RING = 4
EPS_LN = 1e-5
EPS_AT = 1e-6

TAPS = False


class Sem:
    def __init__(self, nc, stack, name):
        self.h = stack.enter_context(nc.semaphore(name))
        self.n = 0

    def inc(self, inst, k=1):
        inst.then_inc(self.h, k)
        self.n += k
        return self.n


def build_nc():
    nc = bass.Bass("TRN2", target_bir_lowering=False)

    x_d = nc.dram_tensor("x", [TC, D], F32, kind="ExternalInput")
    wqkv_d = nc.dram_tensor("w_qkv", [D, 3 * D], F32, kind="ExternalInput")
    wout_d = nc.dram_tensor("w_out", [D, D], F32, kind="ExternalInput")
    bout_d = nc.dram_tensor("b_out", [1, D], F32, kind="ExternalInput")
    wg_d = nc.dram_tensor("wg", [D, 8], F32, kind="ExternalInput")
    w1_d = nc.dram_tensor("w1", [8 * D, HF], F32, kind="ExternalInput")
    w2_d = nc.dram_tensor("w2", [8 * D, HF], F32, kind="ExternalInput")
    w3_d = nc.dram_tensor("w3", [8 * HF, D], F32, kind="ExternalInput")
    ws1_d = nc.dram_tensor("ws1", [D, 2 * HF], F32, kind="ExternalInput")
    ws2_d = nc.dram_tensor("ws2", [D, 2 * HF], F32, kind="ExternalInput")
    ws3_d = nc.dram_tensor("ws3", [2 * HF, D], F32, kind="ExternalInput")
    out_d = nc.dram_tensor("out", [TC, D], F32, kind="ExternalOutput")
    cc_in = nc.dram_tensor("cc_in", [128, 1032], F32, kind="Internal")
    cc_out = nc.dram_tensor("cc_out", [128, 1032], F32, kind="Internal")

    taps = {}
    if TAPS:
        for name, shape in [
            ("t_xn", [128, DC * TC]), ("t_q", [128, DC * TC]),
            ("t_kv", [128, 1032]),
            ("t_at", [128, DC * TC]), ("t_r2", [128, NT * D]),
            ("t_w", [128, NT * 8]), ("t_xn2", [128, DC * TC]),
        ]:
            taps[name] = nc.dram_tensor(name, shape, F32, kind="ExternalOutput")

    def chunked(src_ap, p=128):
        return src_ap.rearrange("(c p) n -> p c n", p=p)

    stack = ExitStack()
    with stack:
        sm = {name: Sem(nc, stack, name) for name in [
            "sx", "swa", "swr", "sw3", "sfree", "sw3f", "sccd", "scc",
            "Lsq", "Lxn", "Lred", "Lrcp", "Lnrm", "Ltr", "Lev",
            "Vmm", "Vev", "Kmm", "Kact", "Kel", "Kkv", "KVac",
            "Qmm", "Qact", "Qel", "KVbld",
            "Dmm", "Drc", "Bmm", "Abc", "Nmm", "Qn",
            "Omm", "Radd",
            "Rmm", "Rev", "Rtr", "Tev", "Tdve", "Tact",
            "Hmm", "Hsil", "Hev", "Ymm", "Yac",
        ]}
        tok = {}

        r_sb = stack.enter_context(nc.sbuf_tensor("r_sb", [128, NT * D], F32))
        xn_fm = stack.enter_context(nc.sbuf_tensor("xn_fm", [128, DC * TC], BF16))
        scr1 = stack.enter_context(nc.sbuf_tensor("scr1", [128, 2 * D], F32))
        xn_tmbf = stack.enter_context(nc.sbuf_tensor("xn_tmbf", [128, 2 * D], BF16))
        cols = stack.enter_context(nc.sbuf_tensor("cols", [128, 16], F32))
        id_bf = stack.enter_context(nc.sbuf_tensor("id_bf", [128, 128], BF16))
        id_f32 = stack.enter_context(nc.sbuf_tensor("id_f32", [8, 8], BF16))
        ones_col = stack.enter_context(nc.sbuf_tensor("ones_col", [128, 1], BF16))
        ones_row = stack.enter_context(nc.sbuf_tensor("ones_row", [1, 128], BF16))
        wg_bf = stack.enter_context(nc.sbuf_tensor("wg_bf", [128, DC * 8], BF16))
        w_tm = stack.enter_context(nc.sbuf_tensor("w_tm", [128, NT * 8], F32))

        def col(par, j):
            return cols[:, 8 * par + j: 8 * par + j + 1]

        def w12_src(which, e, j):
            if e < 8:
                d = w1_d if which == 1 else w2_d
                return chunked(d.ap()[D * e: D * (e + 1), 128 * j: 128 * (j + 1)])
            d = ws1_d if which == 1 else ws2_d
            off = (e - 8) * HF + 128 * j
            return chunked(d.ap()[:, off: off + 128])

        def w3_src(e):
            if e < 8:
                return chunked(w3_d.ap()[HF * e: HF * (e + 1), :])
            off = (e - 8) * HF
            return chunked(ws3_d.ap()[off: off + HF, :])

        def ln_phase(block, trp_ps, src_of_tile, gate_act=None, gate_dve=None):
            lnb = {s: sm[s].n for s in ["Lsq", "Lxn", "Lred", "Lrcp", "Lnrm", "Ltr", "Lev"]}

            @block.scalar
            def _(sc):
                def a1(i):
                    p = i % 2
                    if gate_act is not None:
                        gate_act(sc, i)
                    if i >= 2:
                        sc.wait_ge(sm["Lred"].h, lnb["Lred"] + i - 1)
                    sm["Lsq"].inc(sc.activation(
                        scr1[:, D * p: D * (p + 1)], src_of_tile(i), ACTF.Square), 1)

                def a23(i):
                    p = i % 2
                    sc.wait_ge(sm["Lred"].h, lnb["Lred"] + i + 1)
                    if i >= 2:
                        sc.wait_ge(sm["Lnrm"].h, lnb["Lnrm"] + i - 1)
                    sc.activation(col(p, 5), col(p, 1), ACTF.Ln)
                    sm["Lrcp"].inc(sc.activation(col(p, 3), col(p, 5), ACTF.Exp, scale=-0.5), 1)

                def a4(i):
                    p = i % 2
                    sc.wait_ge(sm["Lnrm"].h, lnb["Lnrm"] + i + 1)
                    if i >= 2:
                        sc.wait_ge(sm["Ltr"].h, lnb["Ltr"] + i - 1)
                    sm["Lxn"].inc(sc.activation(
                        xn_tmbf[:, D * p: D * (p + 1)], src_of_tile(i), ACTF.Identity,
                        bias=col(p, 4), scale=col(p, 3)), 1)

                a1(0)
                a1(1)
                for i in range(NT):
                    a23(i)
                    a4(i)
                    if i + 2 < NT:
                        a1(i + 2)

            @block.vector
            def _(v):
                def reds(i):
                    p = i % 2
                    if gate_dve is not None:
                        gate_dve(v, i)
                    v.reduce_sum(col(p, 0), src_of_tile(i), AX)
                    v.wait_ge(sm["Lsq"].h, lnb["Lsq"] + i + 1)
                    v.reduce_sum(col(p, 1), scr1[:, D * p: D * (p + 1)], AX)
                    v.tensor_scalar_mul(col(p, 0), col(p, 0), 1.0 / D)
                    v.tensor_scalar_mul(col(p, 1), col(p, 1), 1.0 / D)
                    v.tensor_mul(col(p, 2), col(p, 0), col(p, 0))
                    v.tensor_sub(col(p, 1), col(p, 1), col(p, 2))
                    sm["Lred"].inc(v.tensor_scalar_add(col(p, 1), col(p, 1), EPS_LN), 1)

                def nms(i):
                    p = i % 2
                    v.wait_ge(sm["Lrcp"].h, lnb["Lrcp"] + i + 1)
                    if i >= 2:
                        v.wait_ge(sm["Lxn"].h, lnb["Lxn"] + i - 1)
                    sm["Lnrm"].inc(v.scalar_tensor_tensor(
                        col(p, 4), col(p, 0), -1.0, col(p, 3), ALU.mult, ALU.mult), 1)

                def evac(i):
                    p = i % 2
                    v.wait_ge(sm["Ltr"].h, lnb["Ltr"] + i + 1)
                    dst = xn_fm[:, :].rearrange("p (c t) -> p c t", c=DC)[:, :, TS * i: TS * (i + 1)]
                    srcv = trp_ps[p][:, :].rearrange("p (c t) -> p c t", c=DC)
                    sm["Lev"].inc(v.tensor_copy(dst, srcv), 1)

                for i in range(NT):
                    reds(i)
                    nms(i)
                    if i >= 1:
                        evac(i - 1)
                evac(NT - 1)

            @block.tensor
            def _(t):
                for i in range(NT):
                    p = i % 2
                    if i >= 2:
                        t.wait_ge(sm["Lev"].h, lnb["Lev"] + i - 1)
                    t.wait_ge(sm["Lxn"].h, lnb["Lxn"] + i + 1)
                    for c in range(DC):
                        inst = t.transpose(
                            trp_ps[p][:, 128 * c: 128 * (c + 1)],
                            xn_tmbf[:, D * p + 128 * c: D * p + 128 * (c + 1)],
                            id_bf[:, :])
                    sm["Ltr"].inc(inst, 1)

        with nc.Block() as block:
            @block.gpsimd
            def _(g):
                g.memset(id_bf[:, :], 0.0)
                g.affine_select(out=id_bf[:, :], in_=id_bf[:, :],
                                compare_op=ALU.not_equal, fill=1.0, base=0,
                                pattern=[[-1, 128]], channel_multiplier=1)
                g.memset(id_f32[:, :], 0.0)
                g.affine_select(out=id_f32[:, :], in_=id_f32[:, :],
                                compare_op=ALU.not_equal, fill=1.0, base=0,
                                pattern=[[-1, 8]], channel_multiplier=1)
                g.memset(ones_col[:, :], 1.0)
                g.memset(ones_row[:, :], 1.0)
                tok["wg"] = sm["swa"].inc(g.dma_start(
                    wg_bf[:, :].rearrange("p (c n) -> p c n", c=DC),
                    chunked(wg_d.ap())), 16)

            @block.sync
            def _(sy):
                rv = r_sb[:, :].rearrange("p (i d) -> p i d", i=NT)
                xv = x_d.ap().rearrange("(i p) d -> p i d", p=128)
                tok["x0"] = sm["sx"].inc(sy.dma_start(rv[:, 0: NT // 2, :], xv[:, 0: NT // 2, :]), 16)
                tok["x1"] = sm["sx"].inc(sy.dma_start(rv[:, NT // 2: NT, :], xv[:, NT // 2: NT, :]), 16)

        with ExitStack() as attn_scope:
            aec = attn_scope.enter_context
            q_fm = aec(nc.sbuf_tensor("q_fm", [128, DC * TC], BF16))
            wa_buf = aec(nc.sbuf_tensor("wa_buf", [128, DC * D], BF16))
            wb_buf = aec(nc.sbuf_tensor("wb_buf", [128, DC * D], BF16))
            v_bf = aec(nc.sbuf_tensor("v_bf", [128, 2 * D], BF16))
            k_bf = aec(nc.sbuf_tensor("k_bf", [128, 2 * D], BF16))
            scr2 = aec(nc.sbuf_tensor("scr2", [128, 2 * D], F32))
            kv_part = aec(nc.sbuf_tensor("kv_part", [128, 1032], F32))
            kv2_bf = aec(nc.sbuf_tensor("kv2_bf", [128, DC * 128], BF16))
            zb_bf = aec(nc.sbuf_tensor("zb_bf", [128, DC * 16], BF16))
            m_msk = aec(nc.sbuf_tensor("m_msk", [16, DC * 128], BF16))
            recip = aec(nc.sbuf_tensor("recip", [16, 2 * GS], BF16))
            dtmp = aec(nc.sbuf_tensor("dtmp", [16, GS], F32))
            bc_sb = aec(nc.sbuf_tensor("bc_sb", [128, 2 * GS], F32))
            b_bf = aec(nc.sbuf_tensor("b_bf", [1, D], BF16))

            with nc.Block() as block:
                @block.gpsimd
                def _(g):
                    g.memset(m_msk[:, :], 0.0)
                    g.affine_select(
                        out=m_msk[:, :].rearrange("p (c h d) -> p c h d", c=DC, h=2),
                        in_=m_msk[:, :].rearrange("p (c h d) -> p c h d", c=DC, h=2),
                        compare_op=ALU.not_equal, fill=1.0, base=0,
                        pattern=[[-2, DC], [-1, 2], [0, 64]], channel_multiplier=1)
                    tok["wv"] = sm["swa"].inc(g.dma_start(
                        wa_buf[:, :].rearrange("p (c n) -> p c n", c=DC),
                        chunked(wqkv_d.ap()[:, 2 * D: 3 * D])), 16)
                    tok["wk"] = sm["swa"].inc(g.dma_start(
                        wb_buf[:, :].rearrange("p (c n) -> p c n", c=DC),
                        chunked(wqkv_d.ap()[:, D: 2 * D])), 16)
                    tok["bb"] = sm["swa"].inc(g.dma_start(b_bf[:, :], bout_d.ap()), 16)

            with nc.Block() as block, \
                    nc.psum_tensor("trpA", [128, 1024], BF16) as trpA, \
                    nc.psum_tensor("trpB", [128, 1024], BF16) as trpB:

                def gate_act(sc, i):
                    if i == 0:
                        sc.wait_ge(sm["sx"].h, tok["x0"])
                    if i == NT // 2:
                        sc.wait_ge(sm["sx"].h, tok["x1"])

                def gate_dve(v, i):
                    if i == 0:
                        v.wait_ge(sm["sx"].h, tok["x0"])
                    if i == NT // 2:
                        v.wait_ge(sm["sx"].h, tok["x1"])

                ln_phase(block, [trpA, trpB],
                         lambda i: r_sb[:, D * i: D * (i + 1)], gate_act, gate_dve)

            with nc.Block() as block, \
                    nc.psum_tensor("vps", [128, 1024], F32) as v_ps, \
                    nc.psum_tensor("kps", [128, 1024], F32) as k_ps, \
                    nc.psum_tensor("kvtps", [128, 1024], F32) as kvt_ps, \
                    nc.psum_tensor("zps", [128, 8], F32) as z_ps:
                b2 = {s: sm[s].n for s in ["Vmm", "Vev", "Kmm", "Kact", "Kel", "Kkv", "KVac"]}

                @block.tensor
                def _(t):
                    t.wait_ge(sm["swa"].h, tok["wk"])

                    def vmm(i):
                        if i >= 1:
                            t.wait_ge(sm["Vev"].h, b2["Vev"] + i)
                        for c in range(DC):
                            for h in range(2):
                                inst = t.matmul(
                                    v_ps[:, 512 * h: 512 * (h + 1)],
                                    xn_fm[:, TC * c + TS * i: TC * c + TS * (i + 1)],
                                    wa_buf[:, D * c + 512 * h: D * c + 512 * (h + 1)],
                                    start=(c == 0), stop=(c == DC - 1))
                        sm["Vmm"].inc(inst, 1)

                    def kmm(i):
                        if i >= 1:
                            t.wait_ge(sm["Kel"].h, b2["Kel"] + i)
                        for c in range(DC):
                            for h in range(2):
                                inst = t.matmul(
                                    k_ps[:, 512 * h: 512 * (h + 1)],
                                    xn_fm[:, TC * c + TS * i: TC * c + TS * (i + 1)],
                                    wb_buf[:, D * c + 512 * h: D * c + 512 * (h + 1)],
                                    start=(c == 0), stop=(c == DC - 1))
                        sm["Kmm"].inc(inst, 1)

                    def kvmm(i):
                        p = i % 2
                        t.wait_ge(sm["Kel"].h, b2["Kel"] + i + 1)
                        t.wait_ge(sm["Vev"].h, b2["Vev"] + i + 1)
                        if i >= 1:
                            t.wait_ge(sm["KVac"].h, b2["KVac"] + i)
                        for c in range(DC):
                            t.matmul(kvt_ps[:, 128 * c: 128 * (c + 1)],
                                     k_bf[:, D * p + 128 * c: D * p + 128 * (c + 1)],
                                     v_bf[:, D * p + 128 * c: D * p + 128 * (c + 1)],
                                     start=True, stop=True)
                            inst = t.matmul(z_ps[:, c: c + 1],
                                            k_bf[:, D * p + 128 * c: D * p + 128 * (c + 1)],
                                            ones_col[:, :],
                                            start=True, stop=True)
                        sm["Kkv"].inc(inst, 1)

                    for i in range(NT):
                        vmm(i)
                        kmm(i)
                        if i >= 1:
                            kvmm(i - 1)
                    kvmm(NT - 1)

                @block.scalar
                def _(sc):
                    for i in range(NT):
                        p = i % 2
                        sc.wait_ge(sm["Kmm"].h, b2["Kmm"] + i + 1)
                        if i >= 2:
                            sc.wait_ge(sm["Kel"].h, b2["Kel"] + i - 1)
                        sc.activation(scr1[:, D * p: D * (p + 1)], k_ps[:, :],
                                      ACTF.Relu, scale=-1.0)
                        sm["Kact"].inc(sc.activation(
                            scr2[:, D * p: D * (p + 1)], scr1[:, D * p: D * (p + 1)],
                            ACTF.Exp, scale=-1.0), 1)

                @block.vector
                def _(v):
                    v.memset(kv_part[:, :], 0.0)
                    for i in range(NT):
                        p = i % 2
                        v.wait_ge(sm["Vmm"].h, b2["Vmm"] + i + 1)
                        if i >= 2:
                            v.wait_ge(sm["Kkv"].h, b2["Kkv"] + i - 1)
                        sm["Vev"].inc(v.tensor_copy(
                            v_bf[:, D * p: D * (p + 1)], v_ps[:, :]), 1)
                        v.wait_ge(sm["Kact"].h, b2["Kact"] + i + 1)
                        sm["Kel"].inc(v.scalar_tensor_tensor(
                            k_bf[:, D * p: D * (p + 1)], k_ps[:, :], 0.0,
                            scr2[:, D * p: D * (p + 1)], ALU.max, ALU.add), 1)
                        if i >= 1:
                            v.wait_ge(sm["Kkv"].h, b2["Kkv"] + i)
                            v.tensor_add(kv_part[:, 0: 1024], kv_part[:, 0: 1024], kvt_ps[:, :])
                            sm["KVac"].inc(v.tensor_add(
                                kv_part[:, 1024: 1032], kv_part[:, 1024: 1032], z_ps[:, :]), 1)
                    v.wait_ge(sm["Kkv"].h, b2["Kkv"] + NT)
                    v.tensor_add(kv_part[:, 0: 1024], kv_part[:, 0: 1024], kvt_ps[:, :])
                    sm["KVac"].inc(v.tensor_add(
                        kv_part[:, 1024: 1032], kv_part[:, 1024: 1032], z_ps[:, :]), 1)

                @block.gpsimd
                def _(g):
                    g.wait_ge(sm["Vmm"].h, b2["Vmm"] + NT)
                    tok["wq0"] = sm["swa"].inc(g.dma_start(
                        wa_buf[:, 0: 4 * D].rearrange("p (c n) -> p c n", c=4),
                        chunked(wqkv_d.ap()[:, 0: D])[:, 0: 4, :]), 16)
                    tok["wq"] = sm["swa"].inc(g.dma_start(
                        wa_buf[:, 4 * D: 8 * D].rearrange("p (c n) -> p c n", c=4),
                        chunked(wqkv_d.ap()[:, 0: D])[:, 4: 8, :]), 16)
                    g.wait_ge(sm["Kmm"].h, b2["Kmm"] + NT)
                    tok["wo"] = sm["swa"].inc(g.dma_start(
                        wb_buf[:, :].rearrange("p (c n) -> p c n", c=DC),
                        chunked(wout_d.ap())), 16)
                    g.wait_ge(sm["KVac"].h, b2["KVac"] + NT)
                    tok["ccin"] = sm["sccd"].inc(g.dma_start(cc_in.ap(), kv_part[:, :]), 16)
                    g.wait_ge(sm["sccd"].h, tok["ccin"])
                    tok["cc"] = sm["scc"].inc(g.collective_compute(
                        "AllReduce", ALU.add,
                        replica_groups=[[0, 1, 2, 3], [4, 5, 6, 7]],
                        ins=[cc_in.ap()], outs=[cc_out.ap()]), 1)

                if TAPS:
                    @block.gpsimd
                    def _(g):
                        g.wait_ge(sm["Lev"].h, sm["Lev"].n)
                        sm["sx"].inc(g.dma_start(taps["t_xn"].ap(), xn_fm[:, :]), 16)

            with nc.Block() as block, \
                    nc.psum_tensor("qpsA", [128, GS], F32) as qpsA, \
                    nc.psum_tensor("qpsB", [128, GS], F32) as qpsB:
                q_ps = [qpsA, qpsB]
                b2 = {s: sm[s].n for s in ["Qmm", "Qact", "Qel", "KVbld"]}

                @block.gpsimd
                def _(g):
                    g.wait_ge(sm["scc"].h, tok["cc"])
                    tok["ccback"] = sm["sccd"].inc(g.dma_start(kv_part[:, :], cc_out.ap()), 16)

                @block.tensor
                def _(t):
                    t.wait_ge(sm["swa"].h, tok["wq0"])
                    waited_full = [False]
                    for u in range(DC * NGR):
                        s, gidx = divmod(u, NGR)
                        p = u % 2
                        if u >= 2:
                            t.wait_ge(sm["Qel"].h, b2["Qel"] + u - 1)
                        for c in range(DC):
                            if c == 4 and not waited_full[0]:
                                t.wait_ge(sm["swa"].h, tok["wq"])
                                waited_full[0] = True
                            inst = t.matmul(
                                q_ps[p][:, :],
                                wa_buf[:, D * c + TS * s: D * c + TS * (s + 1)],
                                xn_fm[:, TC * c + GS * gidx: TC * c + GS * (gidx + 1)],
                                start=(c == 0), stop=(c == DC - 1))
                        sm["Qmm"].inc(inst, 1)

                @block.scalar
                def _(sc):
                    for u in range(DC * NGR):
                        p = u % 2
                        sc.wait_ge(sm["Qmm"].h, b2["Qmm"] + u + 1)
                        if u >= 2:
                            sc.wait_ge(sm["Qel"].h, b2["Qel"] + u - 1)
                        sc.activation(scr1[:, GS * p: GS * (p + 1)], q_ps[p][:, :],
                                      ACTF.Relu, scale=-1.0)
                        sm["Qact"].inc(sc.activation(
                            scr2[:, GS * p: GS * (p + 1)], scr1[:, GS * p: GS * (p + 1)],
                            ACTF.Exp, scale=-1.0), 1)

                @block.vector
                def _(v):
                    for u in range(DC * NGR):
                        s, gidx = divmod(u, NGR)
                        p = u % 2
                        v.wait_ge(sm["Qact"].h, b2["Qact"] + u + 1)
                        sm["Qel"].inc(v.scalar_tensor_tensor(
                            q_fm[:, TC * s + GS * gidx: TC * s + GS * (gidx + 1)],
                            q_ps[p][:, :], 0.0,
                            scr2[:, GS * p: GS * (p + 1)], ALU.max, ALU.add), 1)
                    v.wait_ge(sm["sccd"].h, tok["ccback"])
                    v.memset(kv2_bf[:, :], 0.0)
                    v.memset(zb_bf[:, :], 0.0)
                    for c in range(DC):
                        v.tensor_copy(kv2_bf[0:64, 128 * c: 128 * c + 64],
                                      kv_part[0:64, 128 * c: 128 * c + 64])
                        v.tensor_copy(kv2_bf[64:128, 128 * c + 64: 128 * c + 128],
                                      kv_part[64:128, 128 * c + 64: 128 * c + 128])
                        v.tensor_copy(zb_bf[0:64, 16 * c + 2 * c: 16 * c + 2 * c + 1],
                                      kv_part[0:64, 1024 + c: 1024 + c + 1])
                        v.tensor_copy(zb_bf[64:128, 16 * c + 2 * c + 1: 16 * c + 2 * c + 2],
                                      kv_part[64:128, 1024 + c: 1024 + c + 1])
                    sm["KVbld"].inc(v.engine_nop(), 1)

                if TAPS:
                    @block.gpsimd
                    def _(g):
                        g.wait_ge(sm["KVbld"].h, sm["KVbld"].n)
                        sm["sx"].inc(g.dma_start(taps["t_kv"].ap(), kv_part[:, :]), 16)

            with nc.Block() as block, \
                    nc.psum_tensor("dpsA", [16, GS], F32) as dpsA, \
                    nc.psum_tensor("dpsB", [16, GS], F32) as dpsB, \
                    nc.psum_tensor("bpsA", [128, GS], F32) as bpsA, \
                    nc.psum_tensor("bpsB", [128, GS], F32) as bpsB, \
                    nc.psum_tensor("npsA", [128, GS], F32) as npsA, \
                    nc.psum_tensor("npsB", [128, GS], F32) as npsB:
                den_ps = [dpsA, dpsB]
                bcp_ps = [bpsA, bpsB]
                num_ps = [npsA, npsB]
                b3 = {s: sm[s].n for s in ["Dmm", "Drc", "Bmm", "Abc", "Nmm", "Qn"]}
                if TAPS:
                    @block.gpsimd
                    def _(g):
                        tok["tq"] = sm["sx"].inc(g.dma_start(taps["t_q"].ap(), q_fm[:, :]), 16)

                @block.tensor
                def _(t):
                    def den(gidx):
                        pg = gidx % 2
                        if gidx >= 2:
                            t.wait_ge(sm["Drc"].h, b3["Drc"] + gidx - 1)
                        for c in range(DC):
                            inst = t.matmul(den_ps[pg][:, :],
                                            zb_bf[:, 16 * c: 16 * (c + 1)],
                                            q_fm[:, TC * c + GS * gidx: TC * c + GS * (gidx + 1)],
                                            start=(c == 0), stop=(c == DC - 1))
                        sm["Dmm"].inc(inst, 1)

                    def bcnum(gidx):
                        t.wait_ge(sm["Drc"].h, b3["Drc"] + gidx + 1)
                        for c in range(DC):
                            u = gidx * DC + c
                            pu = u % 2
                            if u >= 2:
                                t.wait_ge(sm["Abc"].h, b3["Abc"] + u - 1)
                            sm["Bmm"].inc(t.matmul(
                                bcp_ps[pu][:, :],
                                m_msk[:, 128 * c: 128 * (c + 1)],
                                recip[:, GS * (gidx % 2): GS * (gidx % 2 + 1)],
                                start=True, stop=True), 1)
                            if u >= 2:
                                t.wait_ge(sm["Qn"].h, b3["Qn"] + u - 1)
                            sm["Nmm"].inc(t.matmul(
                                num_ps[pu][:, :],
                                kv2_bf[:, 128 * c: 128 * (c + 1)],
                                q_fm[:, TC * c + GS * gidx: TC * c + GS * (gidx + 1)],
                                start=True, stop=True), 1)

                    den(0)
                    for gidx in range(NGR):
                        if gidx + 1 < NGR:
                            den(gidx + 1)
                        bcnum(gidx)

                @block.scalar
                def _(sc):
                    for u in range(NGR * DC):
                        pu = u % 2
                        sc.wait_ge(sm["Bmm"].h, b3["Bmm"] + u + 1)
                        if u >= 2:
                            sc.wait_ge(sm["Qn"].h, b3["Qn"] + u - 1)
                        sm["Abc"].inc(sc.activation(
                            bc_sb[:, GS * pu: GS * (pu + 1)], bcp_ps[pu][:, :],
                            ACTF.Identity), 1)

                @block.vector
                def _(v):
                    if TAPS:
                        v.wait_ge(sm["sx"].h, tok["tq"])
                    for gidx in range(NGR):
                        pg = gidx % 2
                        v.wait_ge(sm["Dmm"].h, b3["Dmm"] + gidx + 1)
                        if gidx >= 2:
                            v.wait_ge(sm["Bmm"].h, b3["Bmm"] + (gidx - 2) * DC + DC)
                        v.tensor_scalar_add(dtmp[:, :], den_ps[pg][:, :], EPS_AT)
                        v.reciprocal(dtmp[:, :], dtmp[:, :])
                        sm["Drc"].inc(v.tensor_copy(
                            recip[:, GS * pg: GS * (pg + 1)], dtmp[:, :]), 1)
                        for c in range(DC):
                            u = gidx * DC + c
                            pu = u % 2
                            v.wait_ge(sm["Nmm"].h, b3["Nmm"] + u + 1)
                            v.wait_ge(sm["Abc"].h, b3["Abc"] + u + 1)
                            sm["Qn"].inc(v.tensor_mul(
                                q_fm[:, TC * c + GS * gidx: TC * c + GS * (gidx + 1)],
                                num_ps[pu][:, :], bc_sb[:, GS * pu: GS * (pu + 1)]), 1)

            with nc.Block() as block, \
                    nc.psum_tensor("opsA", [128, 1024], F32) as opsA, \
                    nc.psum_tensor("opsB", [128, 1024], F32) as opsB:
                op_ps = [opsA, opsB]
                b4 = {s: sm[s].n for s in ["Omm", "Radd"]}
                if TAPS:
                    @block.gpsimd
                    def _(g):
                        sm["sx"].inc(g.dma_start(taps["t_at"].ap(), q_fm[:, :]), 16)

                @block.tensor
                def _(t):
                    t.wait_ge(sm["swa"].h, tok["wo"])
                    t.wait_ge(sm["swa"].h, tok["bb"])
                    for i in range(NT):
                        p = i % 2
                        if i >= 2:
                            t.wait_ge(sm["Radd"].h, b4["Radd"] + i - 1)
                        for c in range(DC):
                            for h in range(2):
                                t.matmul(op_ps[p][:, 512 * h: 512 * (h + 1)],
                                         q_fm[:, TC * c + TS * i: TC * c + TS * (i + 1)],
                                         wb_buf[:, D * c + 512 * h: D * c + 512 * (h + 1)],
                                         start=(c == 0), stop=False)
                        for h in range(2):
                            inst = t.matmul(op_ps[p][:, 512 * h: 512 * (h + 1)],
                                            ones_row[:, :],
                                            b_bf[:, 512 * h: 512 * (h + 1)],
                                            start=False, stop=True)
                        sm["Omm"].inc(inst, 1)

                @block.vector
                def _(v):
                    for i in range(NT):
                        p = i % 2
                        v.wait_ge(sm["Omm"].h, b4["Omm"] + i + 1)
                        sm["Radd"].inc(v.tensor_add(
                            r_sb[:, D * i: D * (i + 1)], r_sb[:, D * i: D * (i + 1)],
                            op_ps[p][:, :]), 1)

        with ExitStack() as ffn_scope:
            fec = ffn_scope.enter_context
            wring = fec(nc.sbuf_tensor("wring", [128, RING * 2048], BF16))
            w3_bf = fec(nc.sbuf_tensor("w3_bf", [128, NJ * D], BF16))
            h_bf = fec(nc.sbuf_tensor("h_bf", [128, NJ * TC], BF16))
            sil_sb = fec(nc.sbuf_tensor("sil_sb", [128, 2 * GS], F32))
            lg_fm = fec(nc.sbuf_tensor("lg_fm", [8, TC], BF16))
            lg_all = fec(nc.sbuf_tensor("lg_all", [128, NT * 8], F32))
            rE1 = fec(nc.sbuf_tensor("rE1", [128, NT * 8], F32))
            rE1n = fec(nc.sbuf_tensor("rE1n", [128, NT * 8], F32))
            rLM = fec(nc.sbuf_tensor("rLM", [128, NT * 8], F32))
            rE2 = fec(nc.sbuf_tensor("rE2", [128, NT * 8], F32))
            rU1 = fec(nc.sbuf_tensor("rU1", [128, NT * 8], F32))
            rm = fec(nc.sbuf_tensor("rm", [128, 5 * NT], F32))

            def emit_ring_slot(g, s_lin):
                e, j = divmod(s_lin, NJ)
                base = (s_lin % RING) * 2048
                sm["swr"].inc(g.dma_start(
                    wring[:, base: base + 1024].rearrange("p (c n) -> p c n", c=DC),
                    w12_src(1, e, j)), 16)
                sm["swr"].inc(g.dma_start(
                    wring[:, base + 1024: base + 2048].rearrange("p (c n) -> p c n", c=DC),
                    w12_src(2, e, j)), 16)

            def emit_w3(g, e):
                sm["sw3"].inc(g.dma_start(
                    w3_bf[:, :].rearrange("p (j n) -> p j n", j=NJ),
                    w3_src(e)), 16)

            with nc.Block() as block, \
                    nc.psum_tensor("tr2A", [128, 1024], BF16) as tr2A, \
                    nc.psum_tensor("tr2B", [128, 1024], BF16) as tr2B:
                if TAPS:
                    @block.sync
                    def _(sy):
                        sm["sx"].inc(sy.dma_start(taps["t_r2"].ap(), r_sb[:, :]), 16)

                @block.gpsimd
                def _(g):
                    emit_w3(g, 0)
                    for s_lin in range(RING):
                        emit_ring_slot(g, s_lin)

                ln_phase(block, [tr2A, tr2B], lambda i: r_sb[:, D * i: D * (i + 1)])

            with nc.Block() as block, \
                    nc.psum_tensor("rtpA", [8, GS], F32) as rtpA, \
                    nc.psum_tensor("rtpB", [8, GS], F32) as rtpB, \
                    nc.psum_tensor("lgtA", [128, 8], BF16) as lgtA, \
                    nc.psum_tensor("lgtB", [128, 8], BF16) as lgtB:
                rt_ps = [rtpA, rtpB]
                lgt_ps = [lgtA, lgtB]
                b5 = {s: sm[s].n for s in ["Rmm", "Rev", "Rtr", "Tev", "Tdve", "Tact"]}
                if TAPS:
                    @block.gpsimd
                    def _(g):
                        sm["sx"].inc(g.dma_start(taps["t_xn2"].ap(), xn_fm[:, :]), 16)

                @block.tensor
                def _(t):
                    t.wait_ge(sm["swa"].h, tok["wg"])
                    for gidx in range(NGR):
                        p = gidx % 2
                        if gidx >= 2:
                            t.wait_ge(sm["Rev"].h, b5["Rev"] + gidx - 1)
                        for c in range(DC):
                            inst = t.matmul(rt_ps[p][:, :],
                                            wg_bf[:, 8 * c: 8 * (c + 1)],
                                            xn_fm[:, TC * c + GS * gidx: TC * c + GS * (gidx + 1)],
                                            start=(c == 0), stop=(c == DC - 1))
                        sm["Rmm"].inc(inst, 1)
                    for i in range(NT):
                        p = i % 2
                        t.wait_ge(sm["Rev"].h, b5["Rev"] + (i // 4) + 1)
                        if i >= 2:
                            t.wait_ge(sm["Tev"].h, b5["Tev"] + i - 1)
                        sm["Rtr"].inc(t.transpose(
                            lgt_ps[p][:, :], lg_fm[:, TS * i: TS * (i + 1)], id_f32[:, :]), 1)

                @block.scalar
                def _(sc):
                    for i in range(NT):
                        p = i % 2
                        sc.wait_ge(sm["Rtr"].h, b5["Rtr"] + i + 1)
                        sm["Tev"].inc(sc.activation(
                            lg_all[:, 8 * i: 8 * (i + 1)], lgt_ps[p][:, :],
                            ACTF.Identity), 1)
                    sc.wait_ge(sm["Tdve"].h, b5["Tdve"] + 1)
                    sm["Tact"].inc(sc.activation(
                        rE1n[:, :], rE1[:, :], ACTF.Identity, scale=-1e30), 1)
                    sc.wait_ge(sm["Tdve"].h, b5["Tdve"] + 2)
                    sc.activation(rm[:, 3 * NT: 4 * NT], rm[:, 2 * NT: 3 * NT], ACTF.Sigmoid)
                    sm["Tact"].inc(sc.activation(
                        rm[:, 4 * NT: 5 * NT], rm[:, 2 * NT: 3 * NT],
                        ACTF.Sigmoid, scale=-1.0), 1)

                @block.vector
                def _(v):
                    def g16(t):
                        return t[:, :].rearrange("p (i e) -> p i e", i=NT)

                    def b16(t, k):
                        return bass.AP(t, k * NT, [[5 * NT, 128], [1, NT], [0, 8]])

                    for gidx in range(NGR):
                        p = gidx % 2
                        v.wait_ge(sm["Rmm"].h, b5["Rmm"] + gidx + 1)
                        sm["Rev"].inc(v.tensor_copy(
                            lg_fm[:, GS * gidx: GS * (gidx + 1)], rt_ps[p][:, :]), 1)
                    m1v = rm[:, 0: NT].rearrange("p (i o) -> p i o", o=1)
                    m2v = rm[:, NT: 2 * NT].rearrange("p (i o) -> p i o", o=1)
                    v.wait_ge(sm["Tev"].h, b5["Tev"] + NT)
                    v.reduce_max(m1v, g16(lg_all), AX)
                    sm["Tdve"].inc(v.tensor_tensor(
                        g16(rE1), g16(lg_all), b16(rm, 0), ALU.is_ge), 1)
                    v.wait_ge(sm["Tact"].h, b5["Tact"] + 1)
                    v.tensor_add(rLM[:, :], lg_all[:, :], rE1n[:, :])
                    v.reduce_max(m2v, g16(rLM), AX)
                    v.tensor_tensor(g16(rE2), g16(rLM), b16(rm, 1), ALU.is_ge)
                    sm["Tdve"].inc(v.tensor_sub(
                        rm[:, 2 * NT: 3 * NT], rm[:, 0: NT], rm[:, NT: 2 * NT]), 1)
                    v.wait_ge(sm["Tact"].h, b5["Tact"] + 2)
                    v.tensor_tensor(g16(rU1), g16(rE1), b16(rm, 3), ALU.mult)
                    v.tensor_tensor(g16(w_tm), g16(rE2), b16(rm, 4), ALU.mult)
                    v.tensor_add(w_tm[:, :], w_tm[:, :], rU1[:, :])

            with nc.Block() as block, \
                    nc.psum_tensor("h1A", [128, GS], F32) as h1A, \
                    nc.psum_tensor("h1B", [128, GS], F32) as h1B, \
                    nc.psum_tensor("h2A", [128, GS], F32) as h2A, \
                    nc.psum_tensor("h2B", [128, GS], F32) as h2B, \
                    nc.psum_tensor("ypsA", [128, 1024], F32) as ypsA, \
                    nc.psum_tensor("ypsB", [128, 1024], F32) as ypsB:
                h1_ps = [h1A, h1B]
                h2_ps = [h2A, h2B]
                y_ps = [ypsA, ypsB]
                b6 = {s: sm[s].n for s in ["Hmm", "Hsil", "Hev", "Ymm", "Yac",
                                           "sfree", "sw3f"]}

                @block.tensor
                def _(t):
                    for e in range(NSLOT):
                        for j in range(NJ):
                            s_lin = e * NJ + j
                            base = (s_lin % RING) * 2048
                            t.wait_ge(sm["swr"].h, 32 * (s_lin + 1))
                            for gidx in range(NGR):
                                u = s_lin * NGR + gidx
                                pu = u % 2
                                if u >= 2:
                                    t.wait_ge(sm["Hev"].h, b6["Hev"] + u - 1)
                                for c in range(DC):
                                    inst = t.matmul(
                                        h1_ps[pu][:, :],
                                        wring[:, base + 128 * c: base + 128 * (c + 1)],
                                        xn_fm[:, TC * c + GS * gidx: TC * c + GS * (gidx + 1)],
                                        start=(c == 0), stop=(c == DC - 1))
                                sm["Hmm"].inc(inst, 1)
                                for c in range(DC):
                                    inst = t.matmul(
                                        h2_ps[pu][:, :],
                                        wring[:, base + 1024 + 128 * c: base + 1024 + 128 * (c + 1)],
                                        xn_fm[:, TC * c + GS * gidx: TC * c + GS * (gidx + 1)],
                                        start=(c == 0), stop=(c == DC - 1))
                                sm["Hmm"].inc(inst, 1)
                            sm["sfree"].inc(t.nop(), 1)
                        t.wait_ge(sm["Hev"].h, b6["Hev"] + (e + 1) * NJ * NGR)
                        t.wait_ge(sm["sw3"].h, 16 * (e + 1))
                        for i in range(NT):
                            m = e * NT + i
                            py = m % 2
                            if m >= 2:
                                t.wait_ge(sm["Yac"].h, b6["Yac"] + m - 1)
                            for j in range(NJ):
                                for h2 in range(2):
                                    inst = t.matmul(
                                        y_ps[py][:, 512 * h2: 512 * (h2 + 1)],
                                        h_bf[:, TC * j + TS * i: TC * j + TS * (i + 1)],
                                        w3_bf[:, D * j + 512 * h2: D * j + 512 * (h2 + 1)],
                                        start=(j == 0), stop=(j == NJ - 1))
                            sm["Ymm"].inc(inst, 1)
                        sm["sw3f"].inc(t.nop(), 1)

                @block.scalar
                def _(sc):
                    for e in range(NSLOT):
                        for j in range(NJ):
                            for gidx in range(NGR):
                                u = (e * NJ + j) * NGR + gidx
                                pu = u % 2
                                sc.wait_ge(sm["Hmm"].h, b6["Hmm"] + 2 * u + 1)
                                if u >= 2:
                                    sc.wait_ge(sm["Hev"].h, b6["Hev"] + u - 1)
                                sm["Hsil"].inc(sc.activation(
                                    sil_sb[:, GS * pu: GS * (pu + 1)], h1_ps[pu][:, :],
                                    ACTF.Silu), 1)

                @block.vector
                def _(v):
                    for e in range(NSLOT):
                        for j in range(NJ):
                            for gidx in range(NGR):
                                u = (e * NJ + j) * NGR + gidx
                                pu = u % 2
                                v.wait_ge(sm["Hmm"].h, b6["Hmm"] + 2 * u + 2)
                                v.wait_ge(sm["Hsil"].h, b6["Hsil"] + u + 1)
                                if u % (NJ * NGR) == 0 and e >= 1:
                                    v.wait_ge(sm["Ymm"].h, b6["Ymm"] + e * NT)
                                sm["Hev"].inc(v.tensor_mul(
                                    h_bf[:, TC * j + GS * gidx: TC * j + GS * (gidx + 1)],
                                    sil_sb[:, GS * pu: GS * (pu + 1)], h2_ps[pu][:, :]), 1)
                        for i in range(NT):
                            m = e * NT + i
                            py = m % 2
                            v.wait_ge(sm["Ymm"].h, b6["Ymm"] + m + 1)
                            rt = r_sb[:, D * i: D * (i + 1)]
                            if e < 8:
                                wcol = w_tm[:, 8 * i + e: 8 * i + e + 1]
                                inst = v.scalar_tensor_tensor(rt, y_ps[py][:, :], wcol, rt,
                                                              ALU.mult, ALU.add)
                            else:
                                inst = v.tensor_add(rt, rt, y_ps[py][:, :])
                            sm["Yac"].inc(inst, 1)

                @block.gpsimd
                def _(g):
                    for e in range(NSLOT):
                        for j in range(NJ):
                            s_lin = e * NJ + j
                            if s_lin >= RING:
                                g.wait_ge(sm["sfree"].h, b6["sfree"] + s_lin - RING + 1)
                                emit_ring_slot(g, s_lin)
                            if j == 5 and e >= 1:
                                g.wait_ge(sm["sw3f"].h, b6["sw3f"] + e)
                                emit_w3(g, e)

                if TAPS:
                    @block.sync
                    def _(sy):
                        sm["sx"].inc(sy.dma_start(taps["t_w"].ap(), w_tm[:, :]), 16)

            with nc.Block() as block:
                @block.sync
                def _(sy):
                    rv = r_sb[:, :].rearrange("p (i d) -> p i d", i=NT)
                    ov = out_d.ap().rearrange("(i p) d -> p i d", p=128)
                    sm["sx"].inc(sy.dma_start(ov, rv), 16)
                    sy.wait_ge(sm["sx"].h, sm["sx"].n)

    return nc


def kernel(L0_attn_x, L0_attn_w_qkv, L0_attn_w_out, L0_attn_b_out,
           L0_ffn_Wg, L0_ffn_W1, L0_ffn_W2, L0_ffn_W3,
           L0_ffn_Ws1, L0_ffn_Ws2, L0_ffn_Ws3):
    B, S, Dm = L0_attn_x.shape
    xf = np.ascontiguousarray(np.asarray(L0_attn_x, dtype=np.float32).reshape(B * S, Dm))
    shards = np.split(xf, NCORE, axis=0)
    common = {
        "w_qkv": np.ascontiguousarray(np.asarray(L0_attn_w_qkv, dtype=np.float32)),
        "w_out": np.ascontiguousarray(np.asarray(L0_attn_w_out, dtype=np.float32)),
        "b_out": np.ascontiguousarray(np.asarray(L0_attn_b_out, dtype=np.float32).reshape(1, Dm)),
        "wg": np.ascontiguousarray(np.asarray(L0_ffn_Wg, dtype=np.float32)),
        "w1": np.ascontiguousarray(np.asarray(L0_ffn_W1, dtype=np.float32).reshape(8 * Dm, HF)),
        "w2": np.ascontiguousarray(np.asarray(L0_ffn_W2, dtype=np.float32).reshape(8 * Dm, HF)),
        "w3": np.ascontiguousarray(np.asarray(L0_ffn_W3, dtype=np.float32).reshape(8 * HF, Dm)),
        "ws1": np.ascontiguousarray(np.asarray(L0_ffn_Ws1, dtype=np.float32)),
        "ws2": np.ascontiguousarray(np.asarray(L0_ffn_Ws2, dtype=np.float32)),
        "ws3": np.ascontiguousarray(np.asarray(L0_ffn_Ws3, dtype=np.float32)),
    }
    in_maps = [dict(common, x=np.ascontiguousarray(s)) for s in shards]
    nc = build_nc()
    res = run_bass_kernel_spmd(nc, in_maps, core_ids=list(range(NCORE)))
    out = np.concatenate([r["out"] for r in res.results], axis=0)
    if TAPS:
        kernel.tap_results = res.results
    return out.reshape(B, S, Dm)


# version 10 (final)
# speedup vs baseline: 1.3230x; 1.3230x over previous
import numpy as np
from contextlib import ExitStack

import concourse.bass as bass
import concourse.mybir as mybir
from concourse.bass_utils import run_bass_kernel_spmd

from bass_rust import IndirectOffsetOnAxis
from concourse.library_config import mlp as MLP_LIB

F32 = mybir.dt.float32
BF16 = mybir.dt.bfloat16
I32 = mybir.dt.int32
I16 = mybir.dt.int16
AX = mybir.AxisListType.X
ALU = mybir.AluOpType
ACTF = mybir.ActivationFunctionType

NCORE = 8
TC, NT, TS = 2048, 16, 128
NGR, GS = 4, 512
D, DC = 1024, 8
HF, NJ = 1408, 11
NSLOT = 10
RING = 3
CE = 1024
NTE, NGE = 8, 2
EPS_LN = 1e-5
EPS_AT = 1e-6

TAPS = False


class Sem:
    def __init__(self, nc, stack, name):
        self.h = stack.enter_context(nc.semaphore(name))
        self.n = 0

    def inc(self, inst, k=1):
        inst.then_inc(self.h, k)
        self.n += k
        return self.n


def build_nc():
    nc = bass.Bass("TRN2", target_bir_lowering=False, detect_race_conditions=False)

    x_d = nc.dram_tensor("x", [TC, D], F32, kind="ExternalInput")
    wqkv_d = nc.dram_tensor("w_qkv", [D, 3 * D], F32, kind="ExternalInput")
    wout_d = nc.dram_tensor("w_out", [D, D], F32, kind="ExternalInput")
    bout_d = nc.dram_tensor("b_out", [1, D], F32, kind="ExternalInput")
    wg_d = nc.dram_tensor("wg", [D, 8], F32, kind="ExternalInput")
    gidx_d = nc.dram_tensor("gidx", [128, 8], I16, kind="ExternalInput")
    tieb_d = nc.dram_tensor("tieb", [128, 8], F32, kind="ExternalInput")
    w1_d = nc.dram_tensor("w1", [8 * D, HF], F32, kind="ExternalInput")
    w2_d = nc.dram_tensor("w2", [8 * D, HF], F32, kind="ExternalInput")
    w3_d = nc.dram_tensor("w3", [8 * HF, D], F32, kind="ExternalInput")
    ws1_d = nc.dram_tensor("ws1", [D, 2 * HF], F32, kind="ExternalInput")
    ws2_d = nc.dram_tensor("ws2", [D, 2 * HF], F32, kind="ExternalInput")
    ws3_d = nc.dram_tensor("ws3", [2 * HF, D], F32, kind="ExternalInput")
    out_d = nc.dram_tensor("out", [TC, D], F32, kind="ExternalOutput")
    cc_in = nc.dram_tensor("cc_in", [128, 1032], F32, kind="Internal")
    cc_out = nc.dram_tensor("cc_out", [128, 1032], F32, kind="Internal")
    xn2_dram = nc.dram_tensor("xn2_dram", [TC, D], BF16, kind="Internal")
    x_slots = nc.dram_tensor("x_slots", [8 * CE, D], BF16, kind="Internal")
    y_slots = nc.dram_tensor("y_slots", [8 * CE, D], BF16, kind="Internal")

    taps = {}
    if TAPS:
        taps["t_lg"] = nc.dram_tensor("t_lg", [128, NT * 8], F32, kind="ExternalOutput")
        taps["t_g1"] = nc.dram_tensor("t_g1", [128, NT], I32, kind="ExternalOutput")
        taps["t_g2"] = nc.dram_tensor("t_g2", [128, NT], I32, kind="ExternalOutput")
        taps["t_p1"] = nc.dram_tensor("t_p1", [128, NT], F32, kind="ExternalOutput")
        taps["t_p2"] = nc.dram_tensor("t_p2", [128, NT], F32, kind="ExternalOutput")
        taps["t_r2"] = nc.dram_tensor("t_r2", [128, NT * D], F32, kind="ExternalOutput")
        taps["t_mT"] = nc.dram_tensor("t_mT", [128, NT * 8], F32, kind="ExternalOutput")
        taps["t_rE1"] = nc.dram_tensor("t_rE1", [128, NT * 8], F32, kind="ExternalOutput")
        taps["t_sT"] = nc.dram_tensor("t_sT", [128, NT * 8], F32, kind="ExternalOutput")

    def chunked(src_ap, p=128):
        return src_ap.rearrange("(c p) n -> p c n", p=p)

    stack = ExitStack()
    with stack:
        sm = {name: Sem(nc, stack, name) for name in [
            "sx", "sx0", "sx1", "swv", "swk", "swg", "swq0", "swq1", "swo", "sbb",
            "sw3", "sfree", "sw3f", "sccd", "scc",
            "Lsq", "Lxn", "Lred", "Lrcp", "Lnrm", "Ltr", "Lev",
            "Vmm", "Vev", "Kmm", "Kact", "Kel", "Kkv", "KVac",
            "Qmm", "Qact", "Qel", "KVbld",
            "Dmm", "Drc", "Bmm", "Abc", "Nmm", "Qn",
            "Omm", "Radd",
            "Rmm", "Rev", "Rtr", "Tev", "Tdve", "Tact", "Sidx", "sgi",
            "Hmm", "Hsil", "Hev", "Ymm", "Yev", "Yac",
            "sgf", "syw", "sgb", "Ybk", "sxn", "Gini", "szz",
            "swr0", "swr1", "swr2",
        ]}
        tok = {}

        r_sb = stack.enter_context(nc.sbuf_tensor("r_sb", [128, NT * D], F32))
        xn_fm = stack.enter_context(nc.sbuf_tensor("xn_fm", [128, DC * TC], BF16))
        scr1_a = stack.enter_context(nc.sbuf_tensor("scr1", [128, 2 * D], F32))
        xn_tmbf_a = stack.enter_context(nc.sbuf_tensor("xn_tmbf", [128, 2 * D], BF16))
        cols = stack.enter_context(nc.sbuf_tensor("cols", [128, 16], F32))
        id_bf = stack.enter_context(nc.sbuf_tensor("id_bf", [128, 128], BF16))
        id_f32 = stack.enter_context(nc.sbuf_tensor("id_f32", [8, 8], BF16))
        ones_col = stack.enter_context(nc.sbuf_tensor("ones_col", [128, 1], BF16))
        ones_row = stack.enter_context(nc.sbuf_tensor("ones_row", [1, 128], BF16))
        wg_bf = stack.enter_context(nc.sbuf_tensor("wg_bf", [128, DC * 8], BF16))

        scr1 = scr1_a
        xn_tmbf = xn_tmbf_a

        def col(par, j):
            return cols[:, 8 * par + j: 8 * par + j + 1]

        def w12_src(which, e, j):
            if e < 8:
                d = w1_d if which == 1 else w2_d
                return chunked(d.ap()[D * e: D * (e + 1), 128 * j: 128 * (j + 1)])
            d = ws1_d if which == 1 else ws2_d
            off = (e - 8) * HF + 128 * j
            return chunked(d.ap()[:, off: off + 128])

        def w3_src(e):
            if e < 8:
                return chunked(w3_d.ap()[HF * e: HF * (e + 1), :])
            off = (e - 8) * HF
            return chunked(ws3_d.ap()[off: off + HF, :])

        def ln_phase(block, trp_ps, src_of_tile, gate_act=None, gate_dve=None,
                     sink_dram=None, scr=None, xtm=None):
            scr1 = scr if scr is not None else scr1_a
            xn_tmbf = xtm if xtm is not None else xn_tmbf_a
            lnb = {s: sm[s].n for s in ["Lsq", "Lxn", "Lred", "Lrcp", "Lnrm", "Ltr",
                                        "Lev", "sxn"]}
            if sink_dram is not None:
                @block.sync
                def _(sy):
                    for i in range(NT):
                        p = i % 2
                        sy.wait_ge(sm["Lxn"].h, lnb["Lxn"] + i + 1)
                        sm["sxn"].inc(sy.dma_start(
                            sink_dram.ap()[128 * i: 128 * (i + 1), :],
                            xn_tmbf[:, D * p: D * (p + 1)]), 16)

            @block.scalar
            def _(sc):
                def a1(i):
                    p = i % 2
                    if gate_act is not None:
                        gate_act(sc, i)
                    if i >= 2:
                        sc.wait_ge(sm["Lred"].h, lnb["Lred"] + i - 1)
                    sm["Lsq"].inc(sc.activation(
                        scr1[:, D * p: D * (p + 1)], src_of_tile(i), ACTF.Square), 1)

                def a23(i):
                    p = i % 2
                    sc.wait_ge(sm["Lred"].h, lnb["Lred"] + i + 1)
                    if i >= 2:
                        sc.wait_ge(sm["Lnrm"].h, lnb["Lnrm"] + i - 1)
                    sc.activation(col(p, 5), col(p, 1), ACTF.Ln)
                    sm["Lrcp"].inc(sc.activation(col(p, 3), col(p, 5), ACTF.Exp, scale=-0.5), 1)

                def a4(i):
                    p = i % 2
                    sc.wait_ge(sm["Lnrm"].h, lnb["Lnrm"] + i + 1)
                    if i >= 2:
                        sc.wait_ge(sm["Ltr"].h, lnb["Ltr"] + i - 1)
                        if sink_dram is not None:
                            sc.wait_ge(sm["sxn"].h, lnb["sxn"] + 16 * (i - 1))
                    sm["Lxn"].inc(sc.activation(
                        xn_tmbf[:, D * p: D * (p + 1)], src_of_tile(i), ACTF.Identity,
                        bias=col(p, 4), scale=col(p, 3)), 1)

                a1(0)
                a1(1)
                for i in range(NT):
                    a23(i)
                    a4(i)
                    if i + 2 < NT:
                        a1(i + 2)

            @block.vector
            def _(v):
                def reds(i):
                    p = i % 2
                    if gate_dve is not None:
                        gate_dve(v, i)
                    v.reduce_sum(col(p, 0), src_of_tile(i), AX)
                    v.wait_ge(sm["Lsq"].h, lnb["Lsq"] + i + 1)
                    v.reduce_sum(col(p, 1), scr1[:, D * p: D * (p + 1)], AX)
                    v.tensor_scalar_mul(col(p, 0), col(p, 0), 1.0 / D)
                    v.tensor_scalar_mul(col(p, 1), col(p, 1), 1.0 / D)
                    v.tensor_mul(col(p, 2), col(p, 0), col(p, 0))
                    v.tensor_sub(col(p, 1), col(p, 1), col(p, 2))
                    sm["Lred"].inc(v.tensor_scalar_add(col(p, 1), col(p, 1), EPS_LN), 1)

                def nms(i):
                    p = i % 2
                    v.wait_ge(sm["Lrcp"].h, lnb["Lrcp"] + i + 1)
                    if i >= 2:
                        v.wait_ge(sm["Lxn"].h, lnb["Lxn"] + i - 1)
                    sm["Lnrm"].inc(v.scalar_tensor_tensor(
                        col(p, 4), col(p, 0), -1.0, col(p, 3), ALU.mult, ALU.mult), 1)

                def evac(i):
                    p = i % 2
                    v.wait_ge(sm["Ltr"].h, lnb["Ltr"] + i + 1)
                    dst = xn_fm[:, :].rearrange("p (c t) -> p c t", c=DC)[:, :, TS * i: TS * (i + 1)]
                    srcv = trp_ps[p][:, :].rearrange("p (c t) -> p c t", c=DC)
                    sm["Lev"].inc(v.tensor_copy(dst, srcv), 1)

                for i in range(NT):
                    reds(i)
                    nms(i)
                    if i >= 1:
                        evac(i - 1)
                evac(NT - 1)

            @block.tensor
            def _(t):
                for i in range(NT):
                    p = i % 2
                    if i >= 2:
                        t.wait_ge(sm["Lev"].h, lnb["Lev"] + i - 1)
                    t.wait_ge(sm["Lxn"].h, lnb["Lxn"] + i + 1)
                    for c in range(DC):
                        inst = t.transpose(
                            trp_ps[p][:, 128 * c: 128 * (c + 1)],
                            xn_tmbf[:, D * p + 128 * c: D * p + 128 * (c + 1)],
                            id_bf[:, :])
                    sm["Ltr"].inc(inst, 1)

        with nc.Block() as block:
            @block.gpsimd
            def _(g):
                sm["Gini"].inc(g.memset(id_bf[:, :], 0.0), 1)
                g.wait_ge(sm["Gini"].h, sm["Gini"].n)
                g.affine_select(out=id_bf[:, :], in_=id_bf[:, :],
                                compare_op=ALU.not_equal, fill=1.0, base=0,
                                pattern=[[-1, 128]], channel_multiplier=1)
                sm["Gini"].inc(g.memset(id_f32[:, :], 0.0), 1)
                g.wait_ge(sm["Gini"].h, sm["Gini"].n)
                g.affine_select(out=id_f32[:, :], in_=id_f32[:, :],
                                compare_op=ALU.not_equal, fill=1.0, base=0,
                                pattern=[[-1, 8]], channel_multiplier=1)
                g.memset(ones_col[:, :], 1.0)
                sm["szz"].inc(g.memset(xn_tmbf_a[:, :], 0.0), 1)
                g.memset(ones_row[:, :], 1.0)
                tok["wg"] = sm["swg"].inc(g.dma_start(
                    wg_bf[:, :].rearrange("p (c n) -> p c n", c=DC),
                    chunked(wg_d.ap())), 16)

            @block.sync
            def _(sy):
                rv = r_sb[:, :].rearrange("p (i d) -> p i d", i=NT)
                xv = x_d.ap().rearrange("(i p) d -> p i d", p=128)
                tok["x0"] = sm["sx0"].inc(sy.dma_start(rv[:, 0: NT // 2, :], xv[:, 0: NT // 2, :]), 16)
                tok["x1"] = sm["sx1"].inc(sy.dma_start(rv[:, NT // 2: NT, :], xv[:, NT // 2: NT, :]), 16)
                sy.wait_ge(sm["szz"].h, 1)
                for z in range(64):
                    tok["zz"] = sm["szz"].inc(sy.dma_start(
                        x_slots.ap()[128 * z: 128 * (z + 1), :],
                        xn_tmbf_a[:, 0: 1024]), 16)

        with ExitStack() as attn_scope:
            aec = attn_scope.enter_context
            q_fm = aec(nc.sbuf_tensor("q_fm", [128, DC * TC], BF16))
            wa_buf = aec(nc.sbuf_tensor("wa_buf", [128, DC * D], BF16))
            wb_buf = aec(nc.sbuf_tensor("wb_buf", [128, DC * D], BF16))
            v_bf = aec(nc.sbuf_tensor("v_bf", [128, 2 * D], BF16))
            k_bf = aec(nc.sbuf_tensor("k_bf", [128, 2 * D], BF16))
            scr2 = aec(nc.sbuf_tensor("scr2", [128, 2 * D], F32))
            kv_part = aec(nc.sbuf_tensor("kv_part", [128, 1032], F32))
            kv2_bf = aec(nc.sbuf_tensor("kv2_bf", [128, DC * 128], BF16))
            zb_bf = aec(nc.sbuf_tensor("zb_bf", [128, DC * 16], BF16))
            m_msk = aec(nc.sbuf_tensor("m_msk", [16, DC * 128], BF16))
            recip = aec(nc.sbuf_tensor("recip", [16, 2 * GS], BF16))
            dtmp = aec(nc.sbuf_tensor("dtmp", [16, GS], F32))
            bc_sb = aec(nc.sbuf_tensor("bc_sb", [128, 2 * GS], F32))
            b_bf = aec(nc.sbuf_tensor("b_bf", [1, D], BF16))

            with nc.Block() as block:
                @block.gpsimd
                def _(g):
                    sm["Gini"].inc(g.memset(m_msk[:, :], 0.0), 1)
                    g.wait_ge(sm["Gini"].h, sm["Gini"].n)
                    g.affine_select(
                        out=m_msk[:, :].rearrange("p (c h d) -> p c h d", c=DC, h=2),
                        in_=m_msk[:, :].rearrange("p (c h d) -> p c h d", c=DC, h=2),
                        compare_op=ALU.not_equal, fill=1.0, base=0,
                        pattern=[[-2, DC], [-1, 2], [0, 64]], channel_multiplier=1)
                    tok["wv"] = sm["swv"].inc(g.dma_start(
                        wa_buf[:, :].rearrange("p (c n) -> p c n", c=DC),
                        chunked(wqkv_d.ap()[:, 2 * D: 3 * D])), 16)
                    tok["wk"] = sm["swk"].inc(g.dma_start(
                        wb_buf[:, :].rearrange("p (c n) -> p c n", c=DC),
                        chunked(wqkv_d.ap()[:, D: 2 * D])), 16)
                    tok["bb"] = sm["sbb"].inc(g.dma_start(b_bf[:, :], bout_d.ap()), 16)

            with nc.Block() as block, \
                    nc.psum_tensor("trpA", [128, 1024], BF16) as trpA, \
                    nc.psum_tensor("trpB", [128, 1024], BF16) as trpB:

                def gate_act(sc, i):
                    if i == 0:
                        sc.wait_ge(sm["sx0"].h, tok["x0"])
                    if i == NT // 2:
                        sc.wait_ge(sm["sx1"].h, tok["x1"])

                def gate_dve(v, i):
                    if i == 0:
                        v.wait_ge(sm["sx0"].h, tok["x0"])
                    if i == NT // 2:
                        v.wait_ge(sm["sx1"].h, tok["x1"])

                ln_phase(block, [trpA, trpB],
                         lambda i: r_sb[:, D * i: D * (i + 1)], gate_act, gate_dve)

            with nc.Block() as block, \
                    nc.psum_tensor("vps", [128, 1024], F32) as v_ps, \
                    nc.psum_tensor("kps", [128, 1024], F32) as k_ps, \
                    nc.psum_tensor("kvtps", [128, 1024], F32) as kvt_ps, \
                    nc.psum_tensor("zps", [128, 8], F32) as z_ps:
                b2 = {s: sm[s].n for s in ["Vmm", "Vev", "Kmm", "Kact", "Kel", "Kkv", "KVac"]}

                @block.tensor
                def _(t):
                    t.wait_ge(sm["swk"].h, tok["wk"])

                    def vmm(i):
                        if i >= 1:
                            t.wait_ge(sm["Vev"].h, b2["Vev"] + i)
                        for c in range(DC):
                            for h in range(2):
                                inst = t.matmul(
                                    v_ps[:, 512 * h: 512 * (h + 1)],
                                    xn_fm[:, TC * c + TS * i: TC * c + TS * (i + 1)],
                                    wa_buf[:, D * c + 512 * h: D * c + 512 * (h + 1)],
                                    start=(c == 0), stop=(c == DC - 1))
                        sm["Vmm"].inc(inst, 1)

                    def kmm(i):
                        if i >= 1:
                            t.wait_ge(sm["Kel"].h, b2["Kel"] + i)
                        for c in range(DC):
                            for h in range(2):
                                inst = t.matmul(
                                    k_ps[:, 512 * h: 512 * (h + 1)],
                                    xn_fm[:, TC * c + TS * i: TC * c + TS * (i + 1)],
                                    wb_buf[:, D * c + 512 * h: D * c + 512 * (h + 1)],
                                    start=(c == 0), stop=(c == DC - 1))
                        sm["Kmm"].inc(inst, 1)

                    def kvmm(i):
                        p = i % 2
                        t.wait_ge(sm["Kel"].h, b2["Kel"] + i + 1)
                        t.wait_ge(sm["Vev"].h, b2["Vev"] + i + 1)
                        if i >= 1:
                            t.wait_ge(sm["KVac"].h, b2["KVac"] + i)
                        for c in range(DC):
                            t.matmul(kvt_ps[:, 128 * c: 128 * (c + 1)],
                                     k_bf[:, D * p + 128 * c: D * p + 128 * (c + 1)],
                                     v_bf[:, D * p + 128 * c: D * p + 128 * (c + 1)],
                                     start=True, stop=True)
                            inst = t.matmul(z_ps[:, c: c + 1],
                                            k_bf[:, D * p + 128 * c: D * p + 128 * (c + 1)],
                                            ones_col[:, :],
                                            start=True, stop=True)
                        sm["Kkv"].inc(inst, 1)

                    for i in range(NT):
                        vmm(i)
                        kmm(i)
                        if i >= 1:
                            kvmm(i - 1)
                    kvmm(NT - 1)

                @block.scalar
                def _(sc):
                    for i in range(NT):
                        p = i % 2
                        sc.wait_ge(sm["Kmm"].h, b2["Kmm"] + i + 1)
                        if i >= 2:
                            sc.wait_ge(sm["Kel"].h, b2["Kel"] + i - 1)
                        sc.activation(scr1[:, D * p: D * (p + 1)], k_ps[:, :],
                                      ACTF.Relu, scale=-1.0)
                        sm["Kact"].inc(sc.activation(
                            scr2[:, D * p: D * (p + 1)], scr1[:, D * p: D * (p + 1)],
                            ACTF.Exp, scale=-1.0), 1)

                @block.vector
                def _(v):
                    v.memset(kv_part[:, :], 0.0)
                    for i in range(NT):
                        p = i % 2
                        v.wait_ge(sm["Vmm"].h, b2["Vmm"] + i + 1)
                        if i >= 2:
                            v.wait_ge(sm["Kkv"].h, b2["Kkv"] + i - 1)
                        sm["Vev"].inc(v.tensor_copy(
                            v_bf[:, D * p: D * (p + 1)], v_ps[:, :]), 1)
                        v.wait_ge(sm["Kact"].h, b2["Kact"] + i + 1)
                        sm["Kel"].inc(v.scalar_tensor_tensor(
                            k_bf[:, D * p: D * (p + 1)], k_ps[:, :], 0.0,
                            scr2[:, D * p: D * (p + 1)], ALU.max, ALU.add), 1)
                        if i >= 1:
                            v.wait_ge(sm["Kkv"].h, b2["Kkv"] + i)
                            v.tensor_add(kv_part[:, 0: 1024], kv_part[:, 0: 1024], kvt_ps[:, :])
                            sm["KVac"].inc(v.tensor_add(
                                kv_part[:, 1024: 1032], kv_part[:, 1024: 1032], z_ps[:, :]), 1)
                    v.wait_ge(sm["Kkv"].h, b2["Kkv"] + NT)
                    v.tensor_add(kv_part[:, 0: 1024], kv_part[:, 0: 1024], kvt_ps[:, :])
                    sm["KVac"].inc(v.tensor_add(
                        kv_part[:, 1024: 1032], kv_part[:, 1024: 1032], z_ps[:, :]), 1)

                @block.gpsimd
                def _(g):
                    g.wait_ge(sm["Vmm"].h, b2["Vmm"] + NT)
                    tok["wq0"] = sm["swq0"].inc(g.dma_start(
                        wa_buf[:, 0: 4 * D].rearrange("p (c n) -> p c n", c=4),
                        chunked(wqkv_d.ap()[:, 0: D])[:, 0: 4, :]), 16)
                    tok["wq"] = sm["swq1"].inc(g.dma_start(
                        wa_buf[:, 4 * D: 8 * D].rearrange("p (c n) -> p c n", c=4),
                        chunked(wqkv_d.ap()[:, 0: D])[:, 4: 8, :]), 16)
                    g.wait_ge(sm["Kmm"].h, b2["Kmm"] + NT)
                    tok["wo"] = sm["swo"].inc(g.dma_start(
                        wb_buf[:, :].rearrange("p (c n) -> p c n", c=DC),
                        chunked(wout_d.ap())), 16)
                    g.wait_ge(sm["KVac"].h, b2["KVac"] + NT)
                    tok["ccin"] = sm["sccd"].inc(g.dma_start(cc_in.ap(), kv_part[:, :]), 16)
                    g.wait_ge(sm["sccd"].h, tok["ccin"])
                    tok["cc"] = sm["scc"].inc(g.collective_compute(
                        "AllReduce", ALU.add,
                        replica_groups=[[0, 1, 2, 3], [4, 5, 6, 7]],
                        ins=[cc_in.ap()], outs=[cc_out.ap()]), 1)



            with nc.Block() as block, \
                    nc.psum_tensor("qpsA", [128, GS], F32) as qpsA, \
                    nc.psum_tensor("qpsB", [128, GS], F32) as qpsB:
                q_ps = [qpsA, qpsB]
                b2 = {s: sm[s].n for s in ["Qmm", "Qact", "Qel", "KVbld"]}

                @block.gpsimd
                def _(g):
                    g.wait_ge(sm["scc"].h, tok["cc"])
                    tok["ccback"] = sm["sccd"].inc(g.dma_start(kv_part[:, :], cc_out.ap()), 16)

                @block.tensor
                def _(t):
                    t.wait_ge(sm["swq0"].h, tok["wq0"])
                    waited_full = [False]
                    for u in range(DC * NGR):
                        s, gidx = divmod(u, NGR)
                        p = u % 2
                        if u >= 2:
                            t.wait_ge(sm["Qel"].h, b2["Qel"] + u - 1)
                        for c in range(DC):
                            if c == 4 and not waited_full[0]:
                                t.wait_ge(sm["swq1"].h, tok["wq"])
                                waited_full[0] = True
                            inst = t.matmul(
                                q_ps[p][:, :],
                                wa_buf[:, D * c + TS * s: D * c + TS * (s + 1)],
                                xn_fm[:, TC * c + GS * gidx: TC * c + GS * (gidx + 1)],
                                start=(c == 0), stop=(c == DC - 1))
                        sm["Qmm"].inc(inst, 1)

                @block.scalar
                def _(sc):
                    for u in range(DC * NGR):
                        p = u % 2
                        sc.wait_ge(sm["Qmm"].h, b2["Qmm"] + u + 1)
                        if u >= 2:
                            sc.wait_ge(sm["Qel"].h, b2["Qel"] + u - 1)
                        sc.activation(scr1[:, GS * p: GS * (p + 1)], q_ps[p][:, :],
                                      ACTF.Relu, scale=-1.0)
                        sm["Qact"].inc(sc.activation(
                            scr2[:, GS * p: GS * (p + 1)], scr1[:, GS * p: GS * (p + 1)],
                            ACTF.Exp, scale=-1.0), 1)

                @block.vector
                def _(v):
                    for u in range(DC * NGR):
                        s, gidx = divmod(u, NGR)
                        p = u % 2
                        v.wait_ge(sm["Qact"].h, b2["Qact"] + u + 1)
                        sm["Qel"].inc(v.scalar_tensor_tensor(
                            q_fm[:, TC * s + GS * gidx: TC * s + GS * (gidx + 1)],
                            q_ps[p][:, :], 0.0,
                            scr2[:, GS * p: GS * (p + 1)], ALU.max, ALU.add), 1)
                    v.wait_ge(sm["sccd"].h, tok["ccback"])
                    v.memset(kv2_bf[:, :], 0.0)
                    v.memset(zb_bf[:, :], 0.0)
                    for c in range(DC):
                        v.tensor_copy(kv2_bf[0:64, 128 * c: 128 * c + 64],
                                      kv_part[0:64, 128 * c: 128 * c + 64])
                        v.tensor_copy(kv2_bf[64:128, 128 * c + 64: 128 * c + 128],
                                      kv_part[64:128, 128 * c + 64: 128 * c + 128])
                        v.tensor_copy(zb_bf[0:64, 16 * c + 2 * c: 16 * c + 2 * c + 1],
                                      kv_part[0:64, 1024 + c: 1024 + c + 1])
                        v.tensor_copy(zb_bf[64:128, 16 * c + 2 * c + 1: 16 * c + 2 * c + 2],
                                      kv_part[64:128, 1024 + c: 1024 + c + 1])
                    sm["KVbld"].inc(v.engine_nop(), 1)



            with nc.Block() as block, \
                    nc.psum_tensor("dpsA", [16, GS], F32) as dpsA, \
                    nc.psum_tensor("dpsB", [16, GS], F32) as dpsB, \
                    nc.psum_tensor("bpsA", [128, GS], F32) as bpsA, \
                    nc.psum_tensor("bpsB", [128, GS], F32) as bpsB, \
                    nc.psum_tensor("npsA", [128, GS], F32) as npsA, \
                    nc.psum_tensor("npsB", [128, GS], F32) as npsB:
                den_ps = [dpsA, dpsB]
                bcp_ps = [bpsA, bpsB]
                num_ps = [npsA, npsB]
                b3 = {s: sm[s].n for s in ["Dmm", "Drc", "Bmm", "Abc", "Nmm", "Qn"]}


                @block.tensor
                def _(t):
                    def den(gidx):
                        pg = gidx % 2
                        if gidx >= 2:
                            t.wait_ge(sm["Drc"].h, b3["Drc"] + gidx - 1)
                        for c in range(DC):
                            inst = t.matmul(den_ps[pg][:, :],
                                            zb_bf[:, 16 * c: 16 * (c + 1)],
                                            q_fm[:, TC * c + GS * gidx: TC * c + GS * (gidx + 1)],
                                            start=(c == 0), stop=(c == DC - 1))
                        sm["Dmm"].inc(inst, 1)

                    def bcnum(gidx):
                        t.wait_ge(sm["Drc"].h, b3["Drc"] + gidx + 1)
                        for c in range(DC):
                            u = gidx * DC + c
                            pu = u % 2
                            if u >= 2:
                                t.wait_ge(sm["Abc"].h, b3["Abc"] + u - 1)
                            sm["Bmm"].inc(t.matmul(
                                bcp_ps[pu][:, :],
                                m_msk[:, 128 * c: 128 * (c + 1)],
                                recip[:, GS * (gidx % 2): GS * (gidx % 2 + 1)],
                                start=True, stop=True), 1)
                            if u >= 2:
                                t.wait_ge(sm["Qn"].h, b3["Qn"] + u - 1)
                            sm["Nmm"].inc(t.matmul(
                                num_ps[pu][:, :],
                                kv2_bf[:, 128 * c: 128 * (c + 1)],
                                q_fm[:, TC * c + GS * gidx: TC * c + GS * (gidx + 1)],
                                start=True, stop=True), 1)

                    den(0)
                    for gidx in range(NGR):
                        if gidx + 1 < NGR:
                            den(gidx + 1)
                        bcnum(gidx)

                @block.scalar
                def _(sc):
                    for u in range(NGR * DC):
                        pu = u % 2
                        sc.wait_ge(sm["Bmm"].h, b3["Bmm"] + u + 1)
                        if u >= 2:
                            sc.wait_ge(sm["Qn"].h, b3["Qn"] + u - 1)
                        sm["Abc"].inc(sc.activation(
                            bc_sb[:, GS * pu: GS * (pu + 1)], bcp_ps[pu][:, :],
                            ACTF.Identity), 1)

                @block.vector
                def _(v):
                    for gidx in range(NGR):
                        pg = gidx % 2
                        v.wait_ge(sm["Dmm"].h, b3["Dmm"] + gidx + 1)
                        if gidx >= 2:
                            v.wait_ge(sm["Bmm"].h, b3["Bmm"] + (gidx - 2) * DC + DC)
                        v.tensor_scalar_add(dtmp[:, :], den_ps[pg][:, :], EPS_AT)
                        v.reciprocal(dtmp[:, :], dtmp[:, :])
                        sm["Drc"].inc(v.tensor_copy(
                            recip[:, GS * pg: GS * (pg + 1)], dtmp[:, :]), 1)
                        for c in range(DC):
                            u = gidx * DC + c
                            pu = u % 2
                            v.wait_ge(sm["Nmm"].h, b3["Nmm"] + u + 1)
                            v.wait_ge(sm["Abc"].h, b3["Abc"] + u + 1)
                            sm["Qn"].inc(v.tensor_mul(
                                q_fm[:, TC * c + GS * gidx: TC * c + GS * (gidx + 1)],
                                num_ps[pu][:, :], bc_sb[:, GS * pu: GS * (pu + 1)]), 1)

            with nc.Block() as block, \
                    nc.psum_tensor("opsA", [128, 1024], F32) as opsA, \
                    nc.psum_tensor("opsB", [128, 1024], F32) as opsB:
                op_ps = [opsA, opsB]
                b4 = {s: sm[s].n for s in ["Omm", "Radd"]}


                @block.tensor
                def _(t):
                    t.wait_ge(sm["swo"].h, tok["wo"])
                    t.wait_ge(sm["sbb"].h, tok["bb"])
                    for i in range(NT):
                        p = i % 2
                        if i >= 2:
                            t.wait_ge(sm["Radd"].h, b4["Radd"] + i - 1)
                        for c in range(DC):
                            for h in range(2):
                                t.matmul(op_ps[p][:, 512 * h: 512 * (h + 1)],
                                         q_fm[:, TC * c + TS * i: TC * c + TS * (i + 1)],
                                         wb_buf[:, D * c + 512 * h: D * c + 512 * (h + 1)],
                                         start=(c == 0), stop=False)
                        for h in range(2):
                            inst = t.matmul(op_ps[p][:, 512 * h: 512 * (h + 1)],
                                            ones_row[:, :],
                                            b_bf[:, 512 * h: 512 * (h + 1)],
                                            start=False, stop=True)
                        sm["Omm"].inc(inst, 1)

                @block.vector
                def _(v):
                    for i in range(NT):
                        p = i % 2
                        v.wait_ge(sm["Omm"].h, b4["Omm"] + i + 1)
                        sm["Radd"].inc(v.tensor_add(
                            r_sb[:, D * i: D * (i + 1)], r_sb[:, D * i: D * (i + 1)],
                            op_ps[p][:, :]), 1)

        wring = stack.enter_context(nc.sbuf_tensor("wring", [128, RING * 2048], BF16))
        w3_bf = stack.enter_context(nc.sbuf_tensor("w3_bf", [128, NJ * D], BF16))
        rm = stack.enter_context(nc.sbuf_tensor("rm", [128, 5 * NT], F32))
        g1_i = stack.enter_context(nc.sbuf_tensor("g1_i", [128, NT], I32))
        g2_i = stack.enter_context(nc.sbuf_tensor("g2_i", [128, NT], I32))

        gidx_sb = stack.enter_context(nc.sbuf_tensor("gidx_sb", [128, 8], I16))
        id8f = stack.enter_context(nc.sbuf_tensor("id8f", [8, 8], F32))
        ec_f = stack.enter_context(nc.sbuf_tensor("ec_f", [128, 8], F32))
        ec2 = stack.enter_context(nc.sbuf_tensor("ec2", [128, 8], F32))

        def emit_ring_slot(g, s_lin):
            e, j = divmod(s_lin, NJ)
            k = s_lin % RING
            base = k * 2048
            sr = sm[f"swr{k}"]
            sr.inc(g.dma_start(
                wring[:, base: base + 1024].rearrange("p (c n) -> p c n", c=DC),
                w12_src(1, e, j)), 16)
            sr.inc(g.dma_start(
                wring[:, base + 1024: base + 2048].rearrange("p (c n) -> p c n", c=DC),
                w12_src(2, e, j)), 16)

        def wait_ring_slot(t, s_lin):
            t.wait_ge(sm[f"swr{s_lin % RING}"].h, 32 * (s_lin // RING + 1))

        def emit_w3(g, e):
            sm["sw3"].inc(g.dma_start(
                w3_bf[:, :].rearrange("p (j n) -> p j n", j=NJ),
                w3_src(e)), 16)

        with ExitStack() as ffn_r:
            fec = ffn_r.enter_context
            lg_all = fec(nc.sbuf_tensor("lg_all", [128, NT * 8], F32))
            lg_fm = fec(nc.sbuf_tensor("lg_fm", [8, TC], BF16))
            rE1 = fec(nc.sbuf_tensor("rE1", [128, NT * 8], BF16))
            rE2 = fec(nc.sbuf_tensor("rE2", [128, NT * 8], BF16))
            rE1n = fec(nc.sbuf_tensor("rE1n", [128, NT * 8], F32))
            rLM = fec(nc.sbuf_tensor("rLM", [128, NT * 8], F32))
            mT = fec(nc.sbuf_tensor("mT", [128, NT * 8], BF16))
            m12f = fec(nc.sbuf_tensor("m12f", [8, TC], F32))
            s_inc = fec(nc.sbuf_tensor("s_inc", [8, TC], F32))
            sT = fec(nc.sbuf_tensor("sT", [128, NT * 8], F32))
            gA = fec(nc.sbuf_tensor("gA", [128, NT * 8], F32))
            gB = fec(nc.sbuf_tensor("gB", [128, NT * 8], F32))

            with nc.Block() as block, \
                    nc.psum_tensor("tr2A", [128, 1024], BF16) as tr2A, \
                    nc.psum_tensor("tr2B", [128, 1024], BF16) as tr2B:
                @block.gpsimd
                def _(g):
                    sm["Gini"].inc(g.memset(id8f[:, :], 0.0), 1)
                    g.wait_ge(sm["Gini"].h, sm["Gini"].n)
                    g.affine_select(out=id8f[:, :], in_=id8f[:, :],
                                    compare_op=ALU.not_equal, fill=1.0, base=0,
                                    pattern=[[-1, 8]], channel_multiplier=1)
                    g.iota(ec_f[:, :], pattern=[[CE, 8]], base=0,
                           channel_multiplier=0, allow_small_or_imprecise_dtypes=True)
                    tok["tb"] = sm["sgi"].inc(g.dma_start(ec2[:, :], tieb_d.ap()), 16)
                    sm["sgi"].inc(g.dma_start(gidx_sb[:, :], gidx_d.ap()), 16)
                    emit_w3(g, 0)
                    for s_lin in range(RING):
                        emit_ring_slot(g, s_lin)

                ln_phase(block, [tr2A, tr2B], lambda i: r_sb[:, D * i: D * (i + 1)],
                         sink_dram=xn2_dram)

            with nc.Block() as block, \
                    nc.psum_tensor("rtpA", [8, GS], F32) as rtpA, \
                    nc.psum_tensor("rtpB", [8, GS], F32) as rtpB, \
                    nc.psum_tensor("lgtA", [128, 8], BF16) as lgtA, \
                    nc.psum_tensor("lgtB", [128, 8], BF16) as lgtB, \
                    nc.psum_tensor("mtp", [8, TC], BF16) as mtp_ps, \
                    nc.psum_tensor("stp", [128, 2 * 8], F32) as stp_ps:
                rt_ps = [rtpA, rtpB]
                lgt_ps = [lgtA, lgtB]
                b5 = {s: sm[s].n for s in ["Rmm", "Rev", "Rtr", "Tev", "Tdve", "Tact",
                                           "Sidx", "sgi"]}

                @block.tensor
                def _(t):
                    t.wait_ge(sm["swg"].h, tok["wg"])
                    for gidx in range(NGR):
                        p = gidx % 2
                        if gidx >= 2:
                            t.wait_ge(sm["Rev"].h, b5["Rev"] + gidx - 1)
                        for c in range(DC):
                            inst = t.matmul(rt_ps[p][:, :],
                                            wg_bf[:, 8 * c: 8 * (c + 1)],
                                            xn_fm[:, TC * c + GS * gidx: TC * c + GS * (gidx + 1)],
                                            start=(c == 0), stop=(c == DC - 1))
                        sm["Rmm"].inc(inst, 1)
                    for i in range(NT):
                        p = i % 2
                        t.wait_ge(sm["Rev"].h, b5["Rev"] + (i // 4) + 1)
                        if i >= 2:
                            t.wait_ge(sm["Tev"].h, b5["Tev"] + i - 1)
                        sm["Rtr"].inc(t.transpose(
                            lgt_ps[p][:, :], lg_fm[:, TS * i: TS * (i + 1)], id_f32[:, :]), 1)
                    t.wait_ge(sm["Tdve"].h, b5["Tdve"] + 3)
                    for i in range(NT):
                        inst = t.transpose(mtp_ps[:, TS * i: TS * (i + 1)],
                                           mT[:, 8 * i: 8 * (i + 1)], id_bf[:, :])
                    sm["Rtr"].inc(inst, 1)
                    t.wait_ge(sm["Tdve"].h, b5["Tdve"] + 4)
                    for i in range(NT):
                        pi = i % 2
                        if i >= 2:
                            t.wait_ge(sm["Tev"].h, b5["Tev"] + NT + i - 1)
                        sm["Rtr"].inc(t.transpose(
                            stp_ps[:, 8 * pi: 8 * (pi + 1)],
                            s_inc[:, TS * i: TS * (i + 1)], id8f[:, :]), 1)

                @block.scalar
                def _(sc):
                    for i in range(NT):
                        p = i % 2
                        sc.wait_ge(sm["Rtr"].h, b5["Rtr"] + i + 1)
                        sm["Tev"].inc(sc.activation(
                            lg_all[:, 8 * i: 8 * (i + 1)], lgt_ps[p][:, :],
                            ACTF.Identity), 1)
                    sc.wait_ge(sm["Tdve"].h, b5["Tdve"] + 1)
                    sm["Tact"].inc(sc.activation(
                        rE1n[:, :], rE1[:, :], ACTF.Identity, scale=-1e30), 1)
                    sc.wait_ge(sm["Tdve"].h, b5["Tdve"] + 2)
                    sc.activation(rm[:, 3 * NT: 4 * NT], rm[:, 2 * NT: 3 * NT], ACTF.Sigmoid)
                    sm["Tact"].inc(sc.activation(
                        rm[:, 4 * NT: 5 * NT], rm[:, 2 * NT: 3 * NT],
                        ACTF.Sigmoid, scale=-1.0), 1)
                    for i in range(NT):
                        pi = i % 2
                        sc.wait_ge(sm["Rtr"].h, b5["Rtr"] + NT + 1 + i + 1)
                        sm["Tev"].inc(sc.activation(
                            sT[:, 8 * i: 8 * (i + 1)], stp_ps[:, 8 * pi: 8 * (pi + 1)],
                            ACTF.Identity), 1)
                    sc.wait_ge(sm["Sidx"].h, b5["Sidx"] + 1)
                    sc.activation(g1_i[:, :], rm[:, 0: NT], ACTF.Identity)
                    sm["Sidx"].inc(sc.activation(
                        g2_i[:, :], rm[:, NT: 2 * NT], ACTF.Identity), 1)

                @block.vector
                def _(v):
                    def g16(t_):
                        return t_[:, :].rearrange("p (i e) -> p i e", i=NT)

                    def b16(t_, k):
                        return bass.AP(t_, k * NT, [[5 * NT, 128], [1, NT], [0, 8]])

                    for gidx in range(NGR):
                        p = gidx % 2
                        v.wait_ge(sm["Rmm"].h, b5["Rmm"] + gidx + 1)
                        sm["Rev"].inc(v.tensor_copy(
                            lg_fm[:, GS * gidx: GS * (gidx + 1)], rt_ps[p][:, :]), 1)
                    m1v = rm[:, 0: NT].rearrange("p (i o) -> p i o", o=1)
                    m2v = rm[:, NT: 2 * NT].rearrange("p (i o) -> p i o", o=1)
                    v.wait_ge(sm["Tev"].h, b5["Tev"] + NT)
                    v.wait_ge(sm["sgi"].h, tok["tb"])
                    ecb2 = bass.AP(ec2, 0, [[8, 128], [0, NT], [1, 8]])
                    v.tensor_tensor(g16(gB), g16(lg_all), ecb2, ALU.add)
                    v.reduce_max(m1v, g16(gB), AX)
                    sm["Tdve"].inc(v.tensor_tensor(
                        g16(rE1), g16(gB), b16(rm, 0), ALU.is_ge), 1)
                    v.wait_ge(sm["Tact"].h, b5["Tact"] + 1)
                    v.tensor_add(rLM[:, :], gB[:, :], rE1n[:, :])
                    v.reduce_max(m2v, g16(rLM), AX)
                    v.tensor_tensor(g16(rE2), g16(rLM), b16(rm, 1), ALU.is_ge)
                    sm["Tdve"].inc(v.tensor_sub(
                        rm[:, 2 * NT: 3 * NT], rm[:, 0: NT], rm[:, NT: 2 * NT]), 1)
                    sm["Tdve"].inc(v.tensor_add(mT[:, :], rE1[:, :], rE2[:, :]), 1)
                    v.wait_ge(sm["Rtr"].h, b5["Rtr"] + NT + 1)
                    v.tensor_copy(m12f[:, :], mtp_ps[:, :])
                    sm["Tdve"].inc(v.tensor_tensor_scan(
                        s_inc[:, :], m12f[:, :], m12f[:, :], 0.0, ALU.add, ALU.bypass), 1)
                    v.wait_ge(sm["Tev"].h, b5["Tev"] + 2 * NT)
                    ecb = bass.AP(ec_f, 0, [[8, 128], [0, NT], [1, 8]])
                    v.tensor_sub(gA[:, :], sT[:, :], mT[:, :])
                    v.tensor_tensor(g16(gA), g16(gA), ecb, ALU.add)
                    v.tensor_mul(gA[:, :], gA[:, :], rE1[:, :])
                    v.tensor_single_scalar(gB[:, :], sT[:, :], 1.0, ALU.subtract)
                    v.tensor_tensor(g16(gB), g16(gB), ecb, ALU.add)
                    v.tensor_mul(gB[:, :], gB[:, :], rE2[:, :])
                    g1f = rm[:, 0: NT].rearrange("p (i o) -> p i o", o=1)
                    g2f = rm[:, NT: 2 * NT].rearrange("p (i o) -> p i o", o=1)
                    v.reduce_sum(g1f, g16(gA), AX)
                    v.reduce_sum(g2f, g16(gB), AX)
                    v.tensor_scalar_min(rm[:, 0: NT], rm[:, 0: NT], float(8 * CE - 1))
                    v.tensor_scalar_max(rm[:, 0: NT], rm[:, 0: NT], 0.0)
                    v.tensor_scalar_min(rm[:, NT: 2 * NT], rm[:, NT: 2 * NT], float(8 * CE - 1))
                    sm["Sidx"].inc(v.tensor_scalar_max(
                        rm[:, NT: 2 * NT], rm[:, NT: 2 * NT], 0.0), 1)

                @block.gpsimd
                def _(g):
                    g.wait_ge(sm["Sidx"].h, b5["Sidx"] + 2)
                    g.wait_ge(sm["sxn"].h, 16 * NT)
                    g.wait_ge(sm["szz"].h, tok["zz"])
                    for i in range(NT):
                        lc = sm["sgi"].inc(g.dma_start(
                            xn_tmbf_a[:, 0: 1024],
                            xn2_dram.ap()[128 * i: 128 * (i + 1), :]), 16)
                        g.wait_ge(sm["sgi"].h, lc)
                        sc1 = sm["sgi"].inc(g.indirect_dma_start(
                            x_slots.ap(),
                            IndirectOffsetOnAxis(ap=g1_i[:, i: i + 1], axis=0),
                            xn_tmbf_a[:, 0: 1024], None), 16)
                        sm["sgi"].inc(g.indirect_dma_start(
                            x_slots.ap(),
                            IndirectOffsetOnAxis(ap=g2_i[:, i: i + 1], axis=0),
                            xn_tmbf_a[:, 0: 1024], None), 16)
                        g.wait_ge(sm["sgi"].h, sm["sgi"].n)
                    if TAPS:
                        sm["sgi"].inc(g.dma_start(taps["t_mT"].ap(), mT[:, :]), 16)
                        sm["sgi"].inc(g.dma_start(taps["t_rE1"].ap(), rE1[:, :]), 16)
                        sm["sgi"].inc(g.dma_start(taps["t_sT"].ap(), sT[:, :]), 16)
                    tok["ti2"] = sm["sgi"].n

        with ExitStack() as ffn_e:
            fec = ffn_e.enter_context
            h_bf = fec(nc.sbuf_tensor("h_bf", [128, NJ * CE], BF16))
            sil_sb = fec(nc.sbuf_tensor("sil_sb", [128, 2 * GS], F32))
            xg = [fec(nc.sbuf_tensor(f"xg{i}", [128, DC * CE], BF16)) for i in range(2)]
            ybf = [fec(nc.sbuf_tensor(f"ybf{i}", [128, D], BF16)) for i in range(2)]

            with nc.Block() as block, \
                    nc.psum_tensor("h1A", [128, GS], F32) as h1A, \
                    nc.psum_tensor("h1B", [128, GS], F32) as h1B, \
                    nc.psum_tensor("h2A", [128, GS], F32) as h2A, \
                    nc.psum_tensor("h2B", [128, GS], F32) as h2B, \
                    nc.psum_tensor("ypsA", [128, 1024], F32) as ypsA, \
                    nc.psum_tensor("ypsB", [128, 1024], F32) as ypsB:
                h1_ps = [h1A, h1B]
                h2_ps = [h2A, h2B]
                y_ps = [ypsA, ypsB]
                b6 = {s: sm[s].n for s in ["Hmm", "Hsil", "Hev", "Ymm", "Yev", "Yac",
                                           "sfree", "sw3f", "sgf", "syw"]}
                NU = NJ * NGE

                def xgv(par):
                    return xg[par][:, :].rearrange("p (b c t) -> p b c t", b=DC, c=DC)

                @block.tensor
                def _(t):
                    for re in range(8):
                        par = re % 2
                        t.wait_ge(sm["sgf"].h, b6["sgf"] + 16 * NTE * (re + 1))
                        for j in range(NJ):
                            s_lin = re * NJ + j
                            base = (s_lin % RING) * 2048
                            wait_ring_slot(t, s_lin)
                            for gi in range(NGE):
                                u = re * NU + j * NGE + gi
                                pu = u % 2
                                if u >= 2:
                                    t.wait_ge(sm["Hev"].h, b6["Hev"] + u - 1)
                                for c in range(DC):
                                    inst = t.matmul(
                                        h1_ps[pu][:, :],
                                        wring[:, base + 128 * c: base + 128 * (c + 1)],
                                        xgv(par)[:, 4 * gi: 4 * (gi + 1), c, :],
                                        start=(c == 0), stop=(c == DC - 1))
                                sm["Hmm"].inc(inst, 1)
                                for c in range(DC):
                                    inst = t.matmul(
                                        h2_ps[pu][:, :],
                                        wring[:, base + 1024 + 128 * c: base + 1024 + 128 * (c + 1)],
                                        xgv(par)[:, 4 * gi: 4 * (gi + 1), c, :],
                                        start=(c == 0), stop=(c == DC - 1))
                                sm["Hmm"].inc(inst, 1)
                            sm["sfree"].inc(t.nop(), 1)
                        t.wait_ge(sm["Hev"].h, b6["Hev"] + NU * (re + 1))
                        t.wait_ge(sm["sw3"].h, 16 * (re + 1))
                        for ti in range(NTE):
                            m = re * NTE + ti
                            py = m % 2
                            if m >= 2:
                                t.wait_ge(sm["Yev"].h, b6["Yev"] + m - 1)
                            for j in range(NJ):
                                for h2 in range(2):
                                    inst = t.matmul(
                                        y_ps[py][:, 512 * h2: 512 * (h2 + 1)],
                                        h_bf[:, CE * j + TS * ti: CE * j + TS * (ti + 1)],
                                        w3_bf[:, D * j + 512 * h2: D * j + 512 * (h2 + 1)],
                                        start=(j == 0), stop=(j == NJ - 1))
                            sm["Ymm"].inc(inst, 1)
                        sm["sw3f"].inc(t.nop(), 1)

                @block.scalar
                def _(sc):
                    for re in range(8):
                        for uu in range(NU):
                            u = re * NU + uu
                            pu = u % 2
                            sc.wait_ge(sm["Hmm"].h, b6["Hmm"] + 2 * u + 1)
                            if u >= 2:
                                sc.wait_ge(sm["Hev"].h, b6["Hev"] + u - 1)
                            sm["Hsil"].inc(sc.activation(
                                sil_sb[:, GS * pu: GS * (pu + 1)], h1_ps[pu][:, :],
                                ACTF.Silu), 1)

                @block.vector
                def _(v):
                    for re in range(8):
                        for uu in range(NU):
                            u = re * NU + uu
                            j, gi = divmod(uu, NGE)
                            pu = u % 2
                            v.wait_ge(sm["Hmm"].h, b6["Hmm"] + 2 * u + 2)
                            v.wait_ge(sm["Hsil"].h, b6["Hsil"] + u + 1)
                            if uu == 0 and re >= 1:
                                v.wait_ge(sm["Ymm"].h, b6["Ymm"] + NTE * re)
                            sm["Hev"].inc(v.tensor_mul(
                                h_bf[:, CE * j + GS * gi: CE * j + GS * (gi + 1)],
                                sil_sb[:, GS * pu: GS * (pu + 1)], h2_ps[pu][:, :]), 1)
                        for ti in range(NTE):
                            m = re * NTE + ti
                            py = m % 2
                            v.wait_ge(sm["Ymm"].h, b6["Ymm"] + m + 1)
                            if m >= 2:
                                v.wait_ge(sm["syw"].h, b6["syw"] + 16 * (m - 1))
                            sm["Yev"].inc(v.tensor_copy(ybf[py][:, :], y_ps[py][:, :]), 1)

                @block.sync
                def _(sy):
                    for m in range(8 * NTE):
                        py = m % 2
                        sy.wait_ge(sm["Yev"].h, b6["Yev"] + m + 1)
                        sm["syw"].inc(sy.dma_start(
                            y_slots.ap()[128 * m: 128 * (m + 1), :], ybf[py][:, :]), 16)

                if TAPS:
                    @block.sync
                    def _(sy2):
                        sy2.wait_ge(sm["sgi"].h, tok["ti2"])
                        sm["sx"].inc(sy2.dma_start(taps["t_lg"].ap(), lg_all[:, :]), 16)
                        sm["sx"].inc(sy2.dma_start(taps["t_g1"].ap(), g1_i[:, :]), 16)
                        sm["sx"].inc(sy2.dma_start(taps["t_g2"].ap(), g2_i[:, :]), 16)
                        sm["sx"].inc(sy2.dma_start(taps["t_p1"].ap(), rm[:, 3 * NT: 4 * NT]), 16)
                        sm["sx"].inc(sy2.dma_start(taps["t_p2"].ap(), rm[:, 4 * NT: 5 * NT]), 16)
                        sm["sx"].inc(sy2.dma_start(taps["t_r2"].ap(), r_sb[:, :]), 16)

                @block.gpsimd
                def _(g):
                    g.load_library(MLP_LIB)
                    g.wait_ge(sm["sgi"].h, tok["ti2"])
                    nireg = ffn_e.enter_context(g.register("nireg"))
                    g.reg_mov(nireg, 128)
                    for re in range(8):
                        par = re % 2
                        if re >= 1:
                            g.wait_ge(sm["Hmm"].h, b6["Hmm"] + 2 * NU * re)
                        for ti in range(NTE):
                            sm["sgf"].inc(g.dma_gather(
                                out_ap=xg[par][:, 1024 * ti: 1024 * (ti + 1)].rearrange(
                                    "p (c t) -> p c t", c=DC),
                                in_ap=x_slots.ap()[128 * (re * NTE + ti):, :],
                                idxs_ap=gidx_sb[:, :],
                                num_idxs=128,
                                num_idxs_reg=nireg,
                                elem_size=1024,
                                transpose=True,
                            ), 16)
                        for j in range(NJ):
                            s_lin = re * NJ + j
                            if s_lin >= RING:
                                g.wait_ge(sm["sfree"].h, b6["sfree"] + s_lin - RING + 1)
                                emit_ring_slot(g, s_lin)
                            if j == 5 and re >= 1:
                                g.wait_ge(sm["sw3f"].h, b6["sw3f"] + re)
                                emit_w3(g, re)

        with ExitStack() as ffn_s:
            fec = ffn_s.enter_context
            h_bf2 = fec(nc.sbuf_tensor("h_bf2", [128, NJ * TC], BF16))
            sil2 = fec(nc.sbuf_tensor("sil2", [128, 2 * GS], F32))
            ybk = [fec(nc.sbuf_tensor(f"ybk{i}", [128, 2 * D], BF16)) for i in range(2)]

            with nc.Block() as block, \
                    nc.psum_tensor("h1A2", [128, GS], F32) as h1A, \
                    nc.psum_tensor("h1B2", [128, GS], F32) as h1B, \
                    nc.psum_tensor("h2A2", [128, GS], F32) as h2A, \
                    nc.psum_tensor("h2B2", [128, GS], F32) as h2B, \
                    nc.psum_tensor("ypsA2", [128, 1024], F32) as ypsA, \
                    nc.psum_tensor("ypsB2", [128, 1024], F32) as ypsB:
                h1_ps = [h1A, h1B]
                h2_ps = [h2A, h2B]
                y_ps = [ypsA, ypsB]
                b7 = {s: sm[s].n for s in ["Hmm", "Hsil", "Hev", "Ymm", "Yac",
                                           "sfree", "sw3f", "sgb", "Ybk", "syw"]}
                NUS = NJ * NGR

                @block.tensor
                def _(t):
                    for si, ss in enumerate([8, 9]):
                        for j in range(NJ):
                            s_lin = ss * NJ + j
                            base = (s_lin % RING) * 2048
                            wait_ring_slot(t, s_lin)
                            for gidx in range(NGR):
                                u = si * NUS + j * NGR + gidx
                                pu = u % 2
                                if u >= 2:
                                    t.wait_ge(sm["Hev"].h, b7["Hev"] + u - 1)
                                for c in range(DC):
                                    inst = t.matmul(
                                        h1_ps[pu][:, :],
                                        wring[:, base + 128 * c: base + 128 * (c + 1)],
                                        xn_fm[:, TC * c + GS * gidx: TC * c + GS * (gidx + 1)],
                                        start=(c == 0), stop=(c == DC - 1))
                                sm["Hmm"].inc(inst, 1)
                                for c in range(DC):
                                    inst = t.matmul(
                                        h2_ps[pu][:, :],
                                        wring[:, base + 1024 + 128 * c: base + 1024 + 128 * (c + 1)],
                                        xn_fm[:, TC * c + GS * gidx: TC * c + GS * (gidx + 1)],
                                        start=(c == 0), stop=(c == DC - 1))
                                sm["Hmm"].inc(inst, 1)
                            sm["sfree"].inc(t.nop(), 1)
                        t.wait_ge(sm["Hev"].h, b7["Hev"] + NUS * (si + 1))
                        t.wait_ge(sm["sw3"].h, 16 * (ss + 1))
                        for i in range(NT):
                            m = si * NT + i
                            py = m % 2
                            if m >= 2:
                                t.wait_ge(sm["Yac"].h, b7["Yac"] + m - 1)
                            for j in range(NJ):
                                for h2 in range(2):
                                    inst = t.matmul(
                                        y_ps[py][:, 512 * h2: 512 * (h2 + 1)],
                                        h_bf2[:, TC * j + TS * i: TC * j + TS * (i + 1)],
                                        w3_bf[:, D * j + 512 * h2: D * j + 512 * (h2 + 1)],
                                        start=(j == 0), stop=(j == NJ - 1))
                            sm["Ymm"].inc(inst, 1)
                        sm["sw3f"].inc(t.nop(), 1)

                @block.scalar
                def _(sc):
                    for u in range(2 * NUS):
                        pu = u % 2
                        sc.wait_ge(sm["Hmm"].h, b7["Hmm"] + 2 * u + 1)
                        if u >= 2:
                            sc.wait_ge(sm["Hev"].h, b7["Hev"] + u - 1)
                        sm["Hsil"].inc(sc.activation(
                            sil2[:, GS * pu: GS * (pu + 1)], h1_ps[pu][:, :],
                            ACTF.Silu), 1)

                @block.vector
                def _(v):
                    def bwd(i):
                        pi = i % 2
                        v.wait_ge(sm["sgb"].h, b7["sgb"] + 32 * (i + 1))
                        rt = r_sb[:, D * i: D * (i + 1)]
                        v.scalar_tensor_tensor(rt, ybk[pi][:, 0: D],
                                               rm[:, 3 * NT + i: 3 * NT + i + 1], rt,
                                               ALU.mult, ALU.add)
                        sm["Ybk"].inc(v.scalar_tensor_tensor(
                            rt, ybk[pi][:, D: 2 * D],
                            rm[:, 4 * NT + i: 4 * NT + i + 1], rt,
                            ALU.mult, ALU.add), 1)

                    nbwd = [0]
                    for si in range(2):
                        for uu in range(NUS):
                            u = si * NUS + uu
                            j, gidx = divmod(uu, NGR)
                            pu = u % 2
                            v.wait_ge(sm["Hmm"].h, b7["Hmm"] + 2 * u + 2)
                            v.wait_ge(sm["Hsil"].h, b7["Hsil"] + u + 1)
                            if uu == 0 and si == 1:
                                v.wait_ge(sm["Ymm"].h, b7["Ymm"] + NT)
                            sm["Hev"].inc(v.tensor_mul(
                                h_bf2[:, TC * j + GS * gidx: TC * j + GS * (gidx + 1)],
                                sil2[:, GS * pu: GS * (pu + 1)], h2_ps[pu][:, :]), 1)
                            if u % 3 == 2 and nbwd[0] < NT:
                                bwd(nbwd[0])
                                nbwd[0] += 1
                        for i in range(NT):
                            m = si * NT + i
                            py = m % 2
                            v.wait_ge(sm["Ymm"].h, b7["Ymm"] + m + 1)
                            rt = r_sb[:, D * i: D * (i + 1)]
                            sm["Yac"].inc(v.tensor_add(rt, rt, y_ps[py][:, :]), 1)
                    while nbwd[0] < NT:
                        bwd(nbwd[0])
                        nbwd[0] += 1

                @block.gpsimd
                def _(g):
                    g.wait_ge(sm["syw"].h, b7["syw"])
                    nb = [0]

                    def bwd_gather(i):
                        pi = i % 2
                        if i >= 2:
                            g.wait_ge(sm["Ybk"].h, b7["Ybk"] + i - 1)
                        sm["sgb"].inc(g.indirect_dma_start(
                            ybk[pi][:, 0: D], None,
                            y_slots.ap(),
                            IndirectOffsetOnAxis(ap=g1_i[:, i: i + 1], axis=0)), 16)
                        sm["sgb"].inc(g.indirect_dma_start(
                            ybk[pi][:, D: 2 * D], None,
                            y_slots.ap(),
                            IndirectOffsetOnAxis(ap=g2_i[:, i: i + 1], axis=0)), 16)

                    ring_list = []
                    for ss in [8, 9]:
                        for j in range(NJ):
                            ring_list.append((ss * NJ + j, ss if j == 5 else None))
                    for s_lin, w3e in ring_list[:RING]:
                        if s_lin >= RING:
                            g.wait_ge(sm["sfree"].h, s_lin - RING + 1)
                        emit_ring_slot(g, s_lin)
                    g.wait_ge(sm["sw3f"].h, 8)
                    emit_w3(g, 8)
                    for s_lin, w3e in ring_list[RING:]:
                        g.wait_ge(sm["sfree"].h, s_lin - RING + 1)
                        emit_ring_slot(g, s_lin)
                        if w3e == 9:
                            g.wait_ge(sm["sw3f"].h, 9)
                            emit_w3(g, 9)
                        for _ in range(2):
                            if nb[0] < NT:
                                bwd_gather(nb[0])
                                nb[0] += 1
                    while nb[0] < NT:
                        bwd_gather(nb[0])
                        nb[0] += 1

        with nc.Block() as block:
            @block.sync
            def _(sy):
                rv = r_sb[:, :].rearrange("p (i d) -> p i d", i=NT)
                ov = out_d.ap().rearrange("(i p) d -> p i d", p=128)
                sm["sx"].inc(sy.dma_start(ov, rv), 16)
                sy.wait_ge(sm["sx"].h, sm["sx"].n)

    mybir.codegen_inst_isa_subclasses(nc)

    return nc


def _gidx_const():
    g = np.zeros((128, 8), np.int16)
    for j in range(128):
        g[j % 16, j // 16] = j
    for band in range(1, 8):
        g[16 * band: 16 * (band + 1), :] = g[0:16, :]
    return g


def kernel(L0_attn_x, L0_attn_w_qkv, L0_attn_w_out, L0_attn_b_out,
           L0_ffn_Wg, L0_ffn_W1, L0_ffn_W2, L0_ffn_W3,
           L0_ffn_Ws1, L0_ffn_Ws2, L0_ffn_Ws3):
    B, S, Dm = L0_attn_x.shape
    xf = np.ascontiguousarray(np.asarray(L0_attn_x, dtype=np.float32).reshape(B * S, Dm))
    shards = np.split(xf, NCORE, axis=0)
    common = {
        "w_qkv": np.ascontiguousarray(np.asarray(L0_attn_w_qkv, dtype=np.float32)),
        "w_out": np.ascontiguousarray(np.asarray(L0_attn_w_out, dtype=np.float32)),
        "b_out": np.ascontiguousarray(np.asarray(L0_attn_b_out, dtype=np.float32).reshape(1, Dm)),
        "wg": np.ascontiguousarray(np.asarray(L0_ffn_Wg, dtype=np.float32)),
        "w1": np.ascontiguousarray(np.asarray(L0_ffn_W1, dtype=np.float32).reshape(8 * Dm, HF)),
        "w2": np.ascontiguousarray(np.asarray(L0_ffn_W2, dtype=np.float32).reshape(8 * Dm, HF)),
        "w3": np.ascontiguousarray(np.asarray(L0_ffn_W3, dtype=np.float32).reshape(8 * HF, Dm)),
        "ws1": np.ascontiguousarray(np.asarray(L0_ffn_Ws1, dtype=np.float32)),
        "ws2": np.ascontiguousarray(np.asarray(L0_ffn_Ws2, dtype=np.float32)),
        "ws3": np.ascontiguousarray(np.asarray(L0_ffn_Ws3, dtype=np.float32)),
        "gidx": _gidx_const(),
        "tieb": np.tile((np.arange(8, dtype=np.float32) * 2.0 ** -20)[None, :],
                        (128, 1)),
    }
    in_maps = [dict(common, x=np.ascontiguousarray(s)) for s in shards]
    nc = build_nc()
    res = run_bass_kernel_spmd(nc, in_maps, core_ids=list(range(NCORE)))
    out = np.concatenate([r["out"] for r in res.results], axis=0)
    if TAPS:
        kernel.tap_results = res.results
    return out.reshape(B, S, Dm)
